# version 19
# baseline (speedup 1.0000x reference)
import os
from contextlib import ExitStack

import numpy as np

import concourse.bass as bass
import concourse.mybir as mybir
from concourse.bass_utils import run_bass_kernel_spmd

F32 = mybir.dt.float32
BF16 = mybir.dt.bfloat16
I16 = mybir.dt.int16
I32 = mybir.dt.int32
AF = mybir.ActivationFunctionType
ALU = mybir.AluOpType
AX = mybir.AxisListType

MAGIC = float(np.float32(1.5 * 2 ** 23))
RSQRT_SEED = 0x5F3759DF
NEG_BIG = float(np.float32(-1.0e38))
EPS = 1e-5

NCORES = 8
B, L, H, II = 32, 2048, 1024, 3072
BPC = int(os.environ.get("ATH_BPC", str(B // NCORES)))
T = BPC * L
HC = H // 128
IC = II // 128
LT = L // 128
NGRP = T // 512
NSL = T // 2048

ENG_SEM = {'tensor': 'sT', 'vector': 'sV', 'scalar': 'sS'}


class Em:

    def __init__(self, which, eng, sems):
        self.which = which
        self.eng = eng
        self.sems = sems
        self.c = {k: 0 for k in sems}

    def _run(self, name, fn, inc=None, k=1):
        inst = fn(self.eng) if self.which == name else None
        if inc is not None:
            self.c[inc] += k
            if inst is not None:
                inst.then_inc(self.sems[inc], k)
        return self.c[inc] if inc else 0

    def dS(self, sem, out, in_):
        return self._run('sync', lambda e: e.dma_start(out=out, in_=in_), sem, 16)

    def dG(self, sem, out, in_):
        return self._run('gpsimd', lambda e: e.dma_start(out=out, in_=in_), sem, 16)

    def t(self, fn, inc=False):
        return self._run('tensor', fn, 'sT' if inc else None)

    def v(self, fn, inc=False):
        return self._run('vector', fn, 'sV' if inc else None)

    def s(self, fn, inc=False):
        return self._run('scalar', fn, 'sS' if inc else None)

    def g(self, fn, inc=None, k=1):
        return self._run('gpsimd', fn, inc, k)

    def wait(self, name, ctr, val):
        if self.which == name and val > 0:
            self.eng.wait_ge(self.sems[ctr], val)

    def vser(self, fn):
        self.wait('vector', 'sV', self.c['sV'])
        return self.v(fn, inc=True)

    def sser(self, fn):
        self.wait('scalar', 'sS', self.c['sS'])
        return self.s(fn, inc=True)

    def barrier(self):
        own = ENG_SEM.get(self.which)
        for ctr in self.sems:
            if ctr != own:
                self.wait(self.which, ctr, self.c[ctr])


def build_program(stage=""):
    nc = bass.Bass("TRN2", target_bir_lowering=False)

    d_x = nc.dram_tensor("x", [T, H], F32, kind="ExternalInput").ap()
    d_wqt = nc.dram_tensor("wqt", [H, H], F32, kind="ExternalInput").ap()
    d_wkt = nc.dram_tensor("wkt", [H, H], F32, kind="ExternalInput").ap()
    d_wvt = nc.dram_tensor("wvt", [H, H], F32, kind="ExternalInput").ap()
    d_bq = nc.dram_tensor("bq2", [128, HC], F32, kind="ExternalInput").ap()
    d_bk = nc.dram_tensor("bk2", [128, HC], F32, kind="ExternalInput").ap()
    d_bv = nc.dram_tensor("bv_row", [1, H], F32, kind="ExternalInput").ap()
    d_wu = nc.dram_tensor("w_up", [II, H], F32, kind="ExternalInput").ap()
    d_wg = nc.dram_tensor("w_gate", [II, H], F32, kind="ExternalInput").ap()
    d_wd = nc.dram_tensor("w_down", [H, II], F32, kind="ExternalInput").ap()
    d_eye = nc.dram_tensor("eye", [128, 128], F32, kind="ExternalInput").ap()
    d_negeye = nc.dram_tensor("negeye", [128, 128], F32, kind="ExternalInput").ap()
    d_y = nc.dram_tensor("y", [T, H], F32, kind="ExternalOutput").ap()

    d_x1 = nc.dram_tensor("x1_sp", [T, H], F32).ap()
    d_ln2 = nc.dram_tensor("ln2_sp", [T, H], F32).ap()
    d_u = nc.dram_tensor("u_sp", [II, T], F32).ap()
    d_g = nc.dram_tensor("g_sp", [II, T], F32).ap()
    d_t = nc.dram_tensor("t_sp", [II, T], F32).ap()
    d_ng = nc.dram_tensor("ng_sp", [II, T], I16).ap()
    d_z = nc.dram_tensor("z_sp", [II, T], F32).ap()
    d_wut = nc.dram_tensor("wut_dq", [HC, 128, II], BF16).ap()
    d_wgt = nc.dram_tensor("wgt_dq", [HC, 128, II], BF16).ap()
    d_wdt = nc.dram_tensor("wdt_dq", [IC, 128, H], BF16).ap()
    d_xq = nc.dram_tensor("xq_sp", [NGRP, 128, HC * 512], BF16).ap()
    d_sc = nc.dram_tensor("sc_rt", [128, 1], F32).ap()
    d_ar_in = [nc.dram_tensor(f"ar{i}_in", [1, 2], F32).ap() for i in range(4)]
    d_ar_out = [nc.dram_tensor(f"ar{i}_out", [1, 2], F32).ap() for i in range(4)]

    with ExitStack() as top:
        E_ = top.enter_context

        sems = {}
        sem_names = ['sT', 'sV', 'sS', 'sG', 'cc', 'qPre', 'qWQ', 'qW', 'qCC',
                     'qWN0', 'qWN1', 'qST0', 'qST1', 'qX0', 'qX1', 'qX2', 'qX3',
                     'qSP0', 'qSP1', 'qLN0', 'qLN1',
                     'qSG0', 'qSG1', 'qSG2', 'qSG3', 'qSG4', 'qSG5',
                     'qCB0', 'qCB1', 'qTS0', 'qTS1',
                     'qTB0', 'qTB1', 'qUB0', 'qUB1',
                     'qTC0', 'qTC1', 'qUC0', 'qUC1', 'qNC0', 'qNC1',
                     'qNT0', 'qNT1', 'qNU0', 'qNU1',
                     'qEN0', 'qEN1', 'qXB0', 'qXB1', 'qY0', 'qY1', 'qXQ0', 'qXQ1', 'qXQS']
        for name in sem_names:
            sems[name] = E_(nc.semaphore(name))

        psS = E_(nc.psum_tensor("psS", [128, 2048], F32))
        psA = E_(nc.psum_tensor("psA", [128, 1024], F32))
        psPT = [E_(nc.psum_tensor(f"psPT{i}", [128, 1024], BF16)) for i in range(2)]

        eye_sb = E_(nc.sbuf_tensor("eye_sb", [128, 128], BF16))
        negeye_sb = E_(nc.sbuf_tensor("negeye_sb", [128, 128], F32))
        bq_sb = E_(nc.sbuf_tensor("bq_sb", [128, HC], F32))
        bk_sb = E_(nc.sbuf_tensor("bk_sb", [128, HC], F32))
        bv_sb = E_(nc.sbuf_tensor("bv_sb", [128, H], BF16))
        cseed = E_(nc.sbuf_tensor("cseed", [128, 4], I32))
        cmagic = E_(nc.sbuf_tensor("cmagic", [128, 4], F32))
        nc.const_aps.register(F32, MAGIC, cmagic[:, 0:1])
        nc.const_aps.register(F32, 0.0, cmagic[:, 1:2])
        nc.const_aps.register(F32, EPS, cmagic[:, 2:3])
        nc.const_aps.register(F32, -MAGIC, cmagic[:, 3:4])
        scl = E_(nc.sbuf_tensor("scl", [128, 16], F32))
        stw = E_(nc.sbuf_tensor("stw", [128, 64], F32))
        row_sb = E_(nc.sbuf_tensor("row_sb", [1, 128], F32))
        one_sb = E_(nc.sbuf_tensor("one_sb", [1, 4], F32))
        m0acc = E_(nc.sbuf_tensor("m0acc", [128, 1], F32))
        umaxa = E_(nc.sbuf_tensor("umaxa", [128, 1], F32))
        gmaxa = E_(nc.sbuf_tensor("gmaxa", [128, 1], F32))
        tmaxa = E_(nc.sbuf_tensor("tmaxa", [128, 1], F32))
        zmaxa = E_(nc.sbuf_tensor("zmaxa", [128, 1], F32))
        redt = E_(nc.sbuf_tensor("redt", [128, 2], F32))

        att = ExitStack()
        A_ = att.enter_context
        wq_sb = A_(nc.sbuf_tensor("wq_sb", [128, HC * H], BF16))
        wk_sb = A_(nc.sbuf_tensor("wk_sb", [128, HC * H], BF16))
        wv_sb = A_(nc.sbuf_tensor("wv_sb", [128, HC * H], BF16))
        q_sb = A_(nc.sbuf_tensor("q_sb", [128, HC * L], BF16))
        k_sb = A_(nc.sbuf_tensor("k_sb", [128, HC * L], BF16))
        v_sb = A_(nc.sbuf_tensor("v_sb", [128, LT * H], BF16))
        lnfm = A_(nc.sbuf_tensor("lnfm", [128, HC * 512], BF16))
        P_sb = A_(nc.sbuf_tensor("P_sb", [128, 2 * 2048], BF16))
        pt_sb = A_(nc.sbuf_tensor("pt_sb", [128, 2048], BF16))
        x1_sb = A_(nc.sbuf_tensor("x1_sb", [128, 2 * H], F32))
        ln2_sb = A_(nc.sbuf_tensor("ln2_sb", [128, 2 * H], F32))
        xbuf = A_(nc.sbuf_tensor("xbuf", [128, 4 * H], F32))
        ln1tm = A_(nc.sbuf_tensor("ln1tm", [128, 2 * H], BF16))
        sqdead = A_(nc.sbuf_tensor("sqdead", [128, H], BF16))
        rmaxb = A_(nc.sbuf_tensor("rmaxb", [128, 2], F32))
        negb = A_(nc.sbuf_tensor("negb", [128, 2], F32))
        rsacc = A_(nc.sbuf_tensor("rsacc", [128, 2], F32))
        recipb = A_(nc.sbuf_tensor("recipb", [128, 2], F32))
        att.close()

        deq = ExitStack()
        D_ = deq.enter_context
        wnat = D_(nc.sbuf_tensor("wnat", [128, 2 * II], F32))
        dq1 = D_(nc.sbuf_tensor("dq1", [128, II], F32))
        dqr = D_(nc.sbuf_tensor("dqr", [128, II], F32))
        dqo = D_(nc.sbuf_tensor("dqo", [128, II], BF16))
        sblk = D_(nc.sbuf_tensor("sblk", [128, II // 32], F32))
        rblk = D_(nc.sbuf_tensor("rblk", [128, II // 32], F32))
        wstg = D_(nc.sbuf_tensor("wstg", [128, IC * H], BF16))
        deq.close()

        ffn = ExitStack()
        F_ = ffn.enter_context
        wu_sb = F_(nc.sbuf_tensor("wu_sb", [128, HC * II], BF16, side='right'))
        wg_sb = F_(nc.sbuf_tensor("wg_sb", [128, HC * II], BF16, side='right'))
        xqfm = F_(nc.sbuf_tensor("xqfm", [128, 2 * HC * 512], BF16, side='right'))
        stg = F_(nc.sbuf_tensor("stg", [128, 6 * 512], F32, side='right'))
        stmx = F_(nc.sbuf_tensor("stmx", [128, 6], F32, side='right'))
        b1t = ExitStack()
        B1_ = b1t.enter_context
        lnbuf = B1_(nc.sbuf_tensor("lnbuf", [128, 2 * H], F32))
        tmpq = B1_(nc.sbuf_tensor("tmpq", [128, H], F32))
        xqtm = B1_(nc.sbuf_tensor("xqtm", [128, 2 * H], BF16))
        b1t.close()
        cd = ExitStack()
        C_ = cd.enter_context
        cbuf = C_(nc.sbuf_tensor("cbuf", [128, 2 * 2048], F32))
        c1 = C_(nc.sbuf_tensor("c1", [128, 2048], F32))
        c2 = C_(nc.sbuf_tensor("c2", [128, 2048], F32))
        ngb = C_(nc.sbuf_tensor("ngb", [128, 2048], F32))
        sigb = C_(nc.sbuf_tensor("sigb", [128, 2 * 2048], F32))
        tbb = C_(nc.sbuf_tensor("tbb", [128, 2048], F32))
        ngq = C_(nc.sbuf_tensor("ngq", [128, 2 * 2048], I16))
        ffn.close()
        cd.close()
        dsc = ExitStack()
        D_2 = dsc.enter_context
        tb2 = D_2(nc.sbuf_tensor("tb2", [128, 2 * 4096], F32))
        ub2 = D_2(nc.sbuf_tensor("ub2", [128, 2 * 4096], F32))
        cb2 = D_2(nc.sbuf_tensor("cb2", [128, 2 * 4096], F32))
        sg2 = D_2(nc.sbuf_tensor("sg2", [128, 2 * 4096], F32))
        nt2 = D_2(nc.sbuf_tensor("nt2", [128, 2 * 4096], F32))
        nu2 = D_2(nc.sbuf_tensor("nu2", [128, 2 * 4096], F32))
        dsc.close()

        eph = ExitStack()
        P2_ = eph.enter_context
        wd_sb = P2_(nc.sbuf_tensor("wd_sb", [128, IC * H], BF16))
        ofm = P2_(nc.sbuf_tensor("ofm", [128, 2 * IC * 512], BF16))
        zb_e = P2_(nc.sbuf_tensor("zb_e", [128, 2 * 4096], F32))
        eo1 = P2_(nc.sbuf_tensor("eo1", [128, 2 * 2048], F32))
        x1b = P2_(nc.sbuf_tensor("x1b", [128, 2 * H], F32))
        y_sb = P2_(nc.sbuf_tensor("y_sb", [128, 2 * H], F32))
        eph.close()

        S0, R0 = scl[:, 0:1], scl[:, 1:2]
        SU, RU = scl[:, 2:3], scl[:, 3:4]
        SG, RG = scl[:, 4:5], scl[:, 5:6]
        ST_, RT_ = scl[:, 6:7], scl[:, 7:8]
        SZ, RZ = scl[:, 8:9], scl[:, 9:10]
        GBUF, GBUF2 = scl[:, 10:11], scl[:, 11:12]
        STSU = scl[:, 13:14]

        def walk(E: Em):
            sn = {}

            def ydump(off, c):
                return d_y.rearrange("a h -> (a h)")[off:off + 128 * c] \
                    .rearrange("(p c) -> p c", p=128)

            E.dS('qW', bq_sb[:], d_bq)
            E.dS('qW', bk_sb[:], d_bk)
            E.dS('qW', negeye_sb[:], d_negeye)
            E.dG('qPre', eye_sb[:], d_eye)
            E.dG('qPre', bv_sb[:], d_bv.broadcast_to([128, H]))
            E.v(lambda e: e.memset(cseed[:], RSQRT_SEED))
            E.v(lambda e: e.memset(cmagic[:, 0:1], MAGIC))
            E.v(lambda e: e.memset(cmagic[:, 1:2], 0.0))
            E.v(lambda e: e.memset(cmagic[:, 2:3], EPS))
            E.v(lambda e: e.memset(cmagic[:, 3:4], -MAGIC))
            E.v(lambda e: e.memset(m0acc[:], 0.0))
            E.v(lambda e: e.memset(umaxa[:], 0.0))
            E.v(lambda e: e.memset(gmaxa[:], 0.0))
            E.v(lambda e: e.memset(tmaxa[:], 0.0))
            E.v(lambda e: e.memset(zmaxa[:], 0.0), inc=True)
            E.barrier()

            def dequant_weight(dw_nat, dw_out, O, K):
                OC, KC, KB = O // 128, K // 128, K // 32
                stq = sn.get('wstg_stored', 0)
                for oc in range(OC):
                    par = oc % 2
                    E.wait('sync', 'sV', sn.get(('dqfree', par), 0))
                    ld = E.dS(f'qWN{par}', wnat[:, par * II:par * II + K],
                              dw_nat[oc * 128:(oc + 1) * 128, :])
                    wn = wnat[:, par * II:par * II + K]
                    wn3 = wn.rearrange("p (nb b) -> p nb b", b=32)
                    E.wait('vector', f'qWN{par}', ld)
                    E.vser(lambda e: e.reduce_max(sblk[:, 0:KB], wn3, AX.X,
                                                  apply_absolute_value=True))
                    E.vser(lambda e: e.tensor_scalar(sblk[:, 0:KB], sblk[:, 0:KB],
                                                     1.0 / 7.0, 1e-12, ALU.mult,
                                                     ALU.max))
                    E.vser(lambda e: e.reciprocal(rblk[:, 0:KB], sblk[:, 0:KB]))
                    rb3 = rblk[:, 0:KB].rearrange("p nb -> p nb ()") \
                        .broadcast_to([128, KB, 32])
                    sb3 = sblk[:, 0:KB].rearrange("p nb -> p nb ()") \
                        .broadcast_to([128, KB, 32])
                    E.vser(lambda e: e.tensor_tensor(
                        dq1[:, 0:K].rearrange("p (nb b) -> p nb b", b=32), wn3, rb3,
                        ALU.mult))
                    sn['dqfree', par] = E.c['sV']
                    dq1v = E.c['sV']
                    E.wait('scalar', 'sV', dq1v)
                    E.sser(lambda e: e.activation(dqr[:, 0:K], dq1[:, 0:K],
                                                  AF.Identity, bias=MAGIC,
                                                  scale=1.0))
                    r1s = E.c['sS']
                    E.wait('vector', 'sS', r1s)
                    E.vser(lambda e: e.scalar_tensor_tensor(
                        dqo[:, 0:K].rearrange("p (nb b) -> p nb b", b=32),
                        dqr[:, 0:K].rearrange("p (nb b) -> p nb b", b=32), MAGIC,
                        sb3, ALU.subtract, ALU.mult))
                    dq_ready = E.c['sV']
                    wst3 = wstg[:, 0:KC * O].rearrange("p (kc o) -> p kc o", o=O)
                    for rnd in range(KC // 8):
                        pp = rnd % 2
                        E.wait('tensor', 'sS', sn.get(('ptev', pp), 0))
                        E.wait('tensor', 'sV', dq_ready)
                        for s in range(8):
                            kc = rnd * 8 + s
                            E.t(lambda e: e.transpose(
                                psPT[pp][:, s * 128:(s + 1) * 128],
                                dqo[:, kc * 128:(kc + 1) * 128],
                                eye_sb[:]), inc=(s == 7))
                        tsn = E.c['sT']
                        E.wait('scalar', 'sT', tsn)
                        if oc == 0 and rnd == 0:
                            E.wait('scalar', 'qST0', stq)
                        E.sser(lambda e: e.activation(
                            wst3[:, rnd * 8:(rnd + 1) * 8, oc * 128:(oc + 1) * 128],
                            psPT[pp][:].rearrange("p (c f) -> p c f", f=128),
                            AF.Identity, bias=0.0, scale=1.0))
                        sn['ptev', pp] = E.c['sS']
                ev_all = E.c['sS']
                E.wait('gpsimd', 'sS', ev_all)
                for kc in range(KC):
                    st = E.dG('qST0', dw_out[kc, :, :], wstg[:, kc * O:(kc + 1) * O])
                sn['wstg_stored'] = E.c['qST0']

            dequant_weight(d_wu, d_wut, II, H)
            dequant_weight(d_wg, d_wgt, II, H)
            dequant_weight(d_wd, d_wdt, H, II)
            E.barrier()

            w2rot = [0]
            w2free = {}
            ptrot = [0]
            ptfree = {}

            def psa_wait(rot):
                ent = w2free.get(rot - 2)
                if ent is not None:
                    E.wait('tensor', ent[0], ent[1])

            ps6rot = [0]
            ps6free = {}

            def ps6(rot):
                k = rot % 6
                if k < 2:
                    return psA[:, k * 512:(k + 1) * 512]
                return psS[:, (k - 2) * 512:(k - 1) * 512]

            def ps6_wait(rot):
                ent = ps6free.get(rot - 6)
                if ent is not None:
                    E.wait('tensor', ent[0], ent[1])

            def qkv_group(b, g):
                for tt4 in range(4):
                    i = g * 4 + tt4
                    gi = b * 16 + i
                    slot = tt4
                    if g >= 1:
                        E.wait('sync', 'sS', sn.get(('ln1', b, i - 4), 0))
                    elif b > 0:
                        E.wait('sync', 'sV', sn.get(('x1ev', b - 1, 12 + tt4), 0))
                    ld = E.dS(f'qX{slot}', xbuf[:, slot * H:(slot + 1) * H],
                              d_x[gi * 128:(gi + 1) * 128, :])
                    sn['ldx', b, i] = ld
                    E.wait('vector', f'qX{slot}', ld)
                    E.v(lambda e: e.reduce_sum(stw[:, 32 + tt4:33 + tt4],
                                               xbuf[:, slot * H:(slot + 1) * H],
                                               AX.X), inc=True)
                    E.wait('scalar', f'qX{slot}', ld)
                    if tt4 == 0:
                        E.wait('scalar', 'sV', sn.get('ch1prev', 0))
                    E.sser(lambda e: e.activation(sqdead[:],
                                                  xbuf[:, slot * H:(slot + 1) * H],
                                                  AF.Square,
                                                  accum_out=stw[:, 36 + tt4:37 + tt4]))
                sn['sq4', b, g] = E.c['sS']
                sn['rs4', b, g] = E.c['sV']
                m2 = stw[:, 40:44]
                esq = stw[:, 44:48]
                var = stw[:, 48:52]
                rstd = stw[:, 52:56]
                negmrs = stw[:, 56:60]
                lnv = stw[:, 60:64]
                E.wait('scalar', 'sV', sn['rs4', b, g])
                E.sser(lambda e: e.activation(m2, stw[:, 32:36], AF.Square,
                                              scale=1.0 / H))
                E.sser(lambda e: e.activation(esq, stw[:, 36:40], AF.Identity,
                                              bias=0.0, scale=1.0 / H))
                sms = E.c['sS']
                E.wait('vector', 'sS', sms)
                E.vser(lambda e: e.tensor_tensor(var, esq, m2, ALU.subtract))
                svv = E.c['sV']
                E.wait('scalar', 'sV', svv)
                E.sser(lambda e: e.activation(lnv, var, AF.Ln, bias=EPS))
                E.sser(lambda e: e.activation(rstd, lnv, AF.Exp, scale=-0.5))
                srs = E.c['sS']
                E.wait('vector', 'sS', srs)
                E.vser(lambda e: e.scalar_tensor_tensor(
                    negmrs, stw[:, 32:36], -1.0 / H, rstd, ALU.mult, ALU.mult))
                sn['ch1', b, g] = E.c['sV']
                sn['ch1prev'] = E.c['sV']
                for tt4 in range(4):
                    i = g * 4 + tt4
                    par = i % 2
                    slot = tt4
                    E.wait('scalar', 'sV', sn['ch1', b, g])
                    E.wait('scalar', 'sT', sn.get(('ln1tr', b, i - 2), 0))
                    E.s(lambda e: e.activation(ln1tm[:, par * H:(par + 1) * H],
                                               xbuf[:, slot * H:(slot + 1) * H],
                                               AF.Identity,
                                               bias=negmrs[:, tt4:tt4 + 1],
                                               scale=rstd[:, tt4:tt4 + 1]), inc=True)
                    sn['ln1', b, i] = E.c['sS']
                    pp = ptrot[0] % 2
                    E.wait('tensor', 'sS', sn['ln1', b, i])
                    if ptrot[0] - 2 in ptfree:
                        E.wait('tensor', 'sV', ptfree[ptrot[0] - 2])
                    for kc in range(HC):
                        E.t(lambda e: e.transpose(
                            psPT[pp][:, kc * 128:(kc + 1) * 128],
                            ln1tm[:, par * H + kc * 128: par * H + (kc + 1) * 128],
                            eye_sb[:]), inc=(kc == HC - 1))
                    tsn = E.c['sT']
                    E.wait('vector', 'sT', tsn)
                    dst3 = lnfm.rearrange("p (kc f) -> p kc f", f=512)[
                        :, :, tt4 * 128:(tt4 + 1) * 128]
                    E.v(lambda e: e.tensor_copy(
                        dst3, psPT[pp][:].rearrange("p (c f) -> p c f", f=128)),
                        inc=True)
                    ptfree[ptrot[0]] = E.c['sV']
                    ptrot[0] += 1
                    sn['ln1tr', b, i] = E.c['sT']
                sn['lnfm', b, g] = E.c['sV']

            def qkv_mms(b, j):
                gbase = 0
                if j == 0:
                    E.wait('tensor', 'qWQ', sn['wqkv', b])
                for (w_sb, bias_sb, dst_sb) in ((wq_sb, bq_sb, q_sb),
                                                (wk_sb, bk_sb, k_sb)):
                    for oc in range(HC):
                        rot = w2rot[0]
                        half = (rot % 2) * 512
                        psa_wait(rot)
                        E.wait('tensor', 'sV', sn['lnfm', b, j])
                        for kc in range(HC):
                            E.t(lambda e: e.matmul(
                                psA[:, half:half + 512],
                                w_sb[:, kc * H + oc * 128: kc * H + oc * 128 + 128],
                                lnfm[:, gbase + kc * 512: gbase + (kc + 1) * 512],
                                start=(kc == 0), stop=(kc == HC - 1)),
                                inc=(kc == HC - 1))
                        tsn = E.c['sT']
                        E.wait('scalar', 'sT', tsn)
                        col = oc * L + j * 512
                        E.s(lambda e: e.activation(dst_sb[:, col:col + 512],
                                                   psA[:, half:half + 512], AF.Identity,
                                                   bias=bias_sb[:, oc:oc + 1],
                                                   scale=1.0), inc=True)
                        w2free[rot] = ('sS', E.c['sS'])
                        w2rot[0] += 1
                for tt4 in range(4):
                    for hb in range(2):
                        rot = w2rot[0]
                        half = (rot % 2) * 512
                        psa_wait(rot)
                        for kc in range(HC):
                            E.t(lambda e: e.matmul(
                                psA[:, half:half + 512],
                                lnfm[:, gbase + kc * 512 + tt4 * 128:
                                     gbase + kc * 512 + tt4 * 128 + 128],
                                wv_sb[:, kc * H + hb * 512: kc * H + hb * 512 + 512],
                                start=(kc == 0), stop=(kc == HC - 1)),
                                inc=(kc == HC - 1))
                        tsn = E.c['sT']
                        E.wait('vector', 'sT', tsn)
                        mt = j * 4 + tt4
                        E.v(lambda e: e.scalar_tensor_tensor(
                            v_sb[:, mt * H + hb * 512: mt * H + hb * 512 + 512],
                            psA[:, half:half + 512], 1.0,
                            bv_sb[:, hb * 512:hb * 512 + 512],
                            ALU.mult, ALU.add), inc=True)
                        w2free[rot] = ('sV', E.c['sV'])
                        w2rot[0] += 1
                sn['mmT', b, j] = E.c['sT']

            def smax_exp(b, lt):
                par = lt % 2
                E.wait('vector', 'sT', sn['sc', b, lt])
                E.wait('vector', 'sS', sn.get(('exp', b, lt - 2), 0))
                dcol = lt * 128
                E.vser(lambda e: e.tensor_tensor(psS[:, dcol:dcol + 128],
                                                 psS[:, dcol:dcol + 128],
                                                 negeye_sb[:], ALU.add))
                E.vser(lambda e: e.reduce_max(rmaxb[:, par:par + 1], psS[:],
                                              AX.X))
                E.vser(lambda e: e.tensor_scalar(negb[:, par:par + 1],
                                                 rmaxb[:, par:par + 1],
                                                 -1.0 / 1024.0, None, ALU.mult))
                sn['smax', b, lt] = E.c['sV']
                E.wait('scalar', 'sV', sn['smax', b, lt])
                ptprev = sn.get(('pt', b, lt - 2), sn.get(('pt', b - 1, 14 + lt), 0))
                E.wait('scalar', 'sT', ptprev)
                E.s(lambda e: e.activation(P_sb[:, par * 2048:(par + 1) * 2048],
                                           psS[:], AF.Exp,
                                           bias=negb[:, par:par + 1],
                                           scale=1.0 / 1024.0,
                                           accum_out=rsacc[:, par:par + 1]),
                    inc=True)
                sn['exp', b, lt] = E.c['sS']

            def scores_stage(b, lt):
                if lt < LT:
                    gi = b * 16 + lt
                    slot = lt % 4
                    if lt >= 4:
                        E.wait('sync', 'sV', sn.get(('x1ev', b, lt - 4), 0))
                    else:
                        E.wait('sync', 'sS', sn.get(('ln1', b, 12 + lt), 0))
                    ld = E.dS(f'qX{slot}', xbuf[:, slot * H:(slot + 1) * H],
                              d_x[gi * 128:(gi + 1) * 128, :])
                    sn['ldxr', b, lt] = ld
                    E.wait('tensor', 'sS', sn.get(('exp', b, lt - 1), 0))
                    E.wait('tensor', 'sV', sn.get(('smax', b, lt - 1), 0))
                    if lt == 0:
                        E.wait('tensor', 'sV', sn['qkvdoneV', b])
                        E.wait('tensor', 'sS', sn['qkvdoneS', b])
                    for oc in range(HC):
                        for nb in range(4):
                            E.t(lambda e: e.matmul(
                                psS[:, nb * 512:(nb + 1) * 512],
                                q_sb[:, oc * L + lt * 128: oc * L + lt * 128 + 128],
                                k_sb[:, oc * L + nb * 512: oc * L + (nb + 1) * 512],
                                start=(oc == 0), stop=(oc == HC - 1)),
                                inc=(oc == HC - 1 and nb == 3))
                    sn['sc', b, lt] = E.c['sT']
                    if lt == 0:
                        smax_exp(b, 0)
                if lt >= 1:
                    j = lt - 1
                    jp = j % 2
                    for rnd in range(2):
                        pp = ptrot[0] % 2
                        E.wait('tensor', 'sS', sn['exp', b, j])
                        if ptrot[0] - 2 in ptfree:
                            E.wait('tensor', 'sV', ptfree[ptrot[0] - 2])
                        for s in range(8):
                            mt = rnd * 8 + s
                            E.t(lambda e: e.transpose(
                                psPT[pp][:, s * 128:(s + 1) * 128],
                                P_sb[:, jp * 2048 + mt * 128:
                                     jp * 2048 + (mt + 1) * 128],
                                eye_sb[:]), inc=(s == 7))
                        tsn = E.c['sT']
                        E.wait('vector', 'sT', tsn)
                        avprev = sn.get(('av', b, j - 1),
                                        sn.get(('av', b - 1, LT - 1), 0))
                        E.wait('vector', 'sT', avprev)
                        E.v(lambda e: e.tensor_copy(
                            pt_sb[:, rnd * 1024:(rnd + 1) * 1024],
                            psPT[pp][:]), inc=True)
                        ptfree[ptrot[0]] = E.c['sV']
                        ptrot[0] += 1
                    sn['pt', b, j] = E.c['sT']
                    ptall = E.c['sV']
                    E.wait('vector', 'sS', sn['exp', b, j])
                    E.v(lambda e: e.reciprocal(recipb[:, jp:jp + 1],
                                               rsacc[:, jp:jp + 1]), inc=True)
                    sn['recip', b, j] = E.c['sV']
                    if lt < LT:
                        smax_exp(b, lt)
                    E.wait('tensor', 'sV', ptall)
                    E.wait('tensor', 'sV', sn.get(('x1ev', b, j - 1),
                                                  sn.get(('x1ev', b - 1, LT - 1), 0)))
                    for hb in range(2):
                        for mt in range(LT):
                            E.t(lambda e: e.matmul(
                                psA[:, hb * 512:(hb + 1) * 512],
                                pt_sb[:, mt * 128:(mt + 1) * 128],
                                v_sb[:, mt * H + hb * 512: mt * H + hb * 512 + 512],
                                start=(mt == 0), stop=(mt == LT - 1)),
                                inc=(mt == LT - 1 and hb == 1))
                    sn['av', b, j] = E.c['sT']
                    jslot = j % 4
                    E.wait('vector', 'sT', sn['av', b, j])
                    E.wait('vector', 'sV', sn['recip', b, j])
                    E.wait('vector', f'qX{jslot}', sn['ldxr', b, j])
                    E.wait('vector', f'qSP{jp}', sn.get(('spill', b, j - 2), 0))
                    for hb in range(2):
                        E.v(lambda e: e.scalar_tensor_tensor(
                            x1_sb[:, jp * H + hb * 512: jp * H + hb * 512 + 512],
                            psA[:, hb * 512:(hb + 1) * 512],
                            recipb[:, jp:jp + 1],
                            xbuf[:, jslot * H + hb * 512: jslot * H + hb * 512 + 512],
                            ALU.mult, ALU.add), inc=(hb == 1))
                    sn['x1ev', b, j] = E.c['sV']
                    x1t = x1_sb[:, jp * H:(jp + 1) * H]
                    E.wait('vector', 'sV', sn['x1ev', b, j])
                    E.v(lambda e: e.reduce_sum(stw[:, 16:17], x1t, AX.X), inc=True)
                    E.wait('scalar', 'sV', sn['x1ev', b, j])
                    E.sser(lambda e: e.activation(sqdead[:], x1t, AF.Square,
                                                  accum_out=stw[:, 17:18]))
                    sn['sq2', b, j] = E.c['sS']
                    sn['rs2', b, j] = E.c['sV']
                    m22, esq2, var2 = stw[:, 18:19], stw[:, 19:20], stw[:, 20:21]
                    lnv2, rstd2, negm2 = stw[:, 21:22], stw[:, 22:23], stw[:, 23:24]
                    E.wait('scalar', 'sV', sn['rs2', b, j])
                    E.sser(lambda e: e.activation(m22, stw[:, 16:17], AF.Square,
                                                  scale=1.0 / H))
                    E.sser(lambda e: e.activation(esq2, stw[:, 17:18], AF.Identity,
                                                  bias=0.0, scale=1.0 / H))
                    sms2 = E.c['sS']
                    E.wait('vector', 'sS', sms2)
                    E.vser(lambda e: e.tensor_tensor(var2, esq2, m22, ALU.subtract))
                    svv2 = E.c['sV']
                    E.wait('scalar', 'sV', svv2)
                    E.sser(lambda e: e.activation(lnv2, var2, AF.Ln, bias=EPS))
                    E.sser(lambda e: e.activation(rstd2, lnv2, AF.Exp, scale=-0.5))
                    srs2 = E.c['sS']
                    E.wait('vector', 'sS', srs2)
                    E.vser(lambda e: e.scalar_tensor_tensor(
                        negm2, stw[:, 16:17], -1.0 / H, rstd2, ALU.mult, ALU.mult))
                    sn['ch2', b, j] = E.c['sV']
                    E.wait('scalar', 'sV', sn['ch2', b, j])
                    E.wait('scalar', f'qSP{jp}', sn.get(('spill', b, j - 2), 0))
                    E.s(lambda e: e.activation(ln2_sb[:, jp * H:(jp + 1) * H], x1t,
                                               AF.Identity, bias=negm2, scale=rstd2),
                        inc=True)
                    sn['ln2', b, j] = E.c['sS']
                    E.wait('vector', 'sS', sn['ln2', b, j])
                    E.vser(lambda e: e.reduce_max(stw[:, 28:29],
                                                  ln2_sb[:, jp * H:(jp + 1) * H],
                                                  AX.X, apply_absolute_value=True))
                    E.vser(lambda e: e.tensor_tensor(m0acc[:], m0acc[:],
                                                     stw[:, 28:29], ALU.max))
                    gi = b * 16 + j
                    E.wait('gpsimd', 'sV', sn['x1ev', b, j])
                    E.dG(f'qSP{jp}', d_x1[gi * 128:(gi + 1) * 128, :], x1t)
                    E.wait('gpsimd', 'sS', sn['ln2', b, j])
                    sp = E.dG(f'qSP{jp}', d_ln2[gi * 128:(gi + 1) * 128, :],
                              ln2_sb[:, jp * H:(jp + 1) * H])
                    sn['spill', b, j] = sp
                    if stage == 'dbg2' and b == 0 and j == 0:
                        E.wait('gpsimd', 'sS', E.c['sS'])
                        E.wait('gpsimd', 'sV', E.c['sV'])
                        off = [0]

                        def nxt(c):
                            o = off[0]
                            off[0] += 128 * c
                            return ydump(o, c)
                        E.dG('qPre', nxt(2048), P_sb[:, 0:2048])
                        E.dG('qPre', nxt(2048), pt_sb[:])
                        E.dG('qPre', nxt(32), stw[:, 0:32])
                        E.dG('qPre', nxt(2), rmaxb[:])
                        E.dG('qPre', nxt(2), rsacc[:])
                        E.dG('qPre', nxt(2), recipb[:])
                        E.dG('qPre', nxt(2048), x1_sb[:])
                    if stage in ('x1', 'ln2'):
                        if stage == 'x1':
                            E.wait('sync', 'sV', sn['x1ev', b, j])
                            E.dS('qY0', d_y[gi * 128:(gi + 1) * 128, :], x1t)
                        else:
                            E.wait('sync', 'sS', sn['ln2', b, j])
                            E.dS('qY0', d_y[gi * 128:(gi + 1) * 128, :],
                                 ln2_sb[:, jp * H:(jp + 1) * H])

            for b in range(BPC):
                if b == 0:
                    for w_sb, dsrc in ((wq_sb, d_wqt), (wk_sb, d_wkt),
                                       (wv_sb, d_wvt)):
                        lw = E.dG('qWQ',
                                  w_sb[:].rearrange("p (c o) -> p c o", c=HC),
                                  dsrc.rearrange("(c p) o -> p c o", p=128))
                sn['wqkv', b] = lw
                for g in range(5):
                    if g >= 1:
                        qkv_mms(b, g - 1)
                    if g < 4:
                        qkv_group(b, g)
                sn['qkvdoneS', b] = E.c['sS']
                sn['qkvdoneV', b] = E.c['sV']
                if stage == 'dbg1' and b == 0:
                    E.wait('gpsimd', 'sS', E.c['sS'])
                    E.wait('gpsimd', 'sV', E.c['sV'])
                    if T * H >= 4 * 128 * 16384:
                        E.dG('qPre', ydump(0, 16384), q_sb[:])
                        E.dG('qPre', ydump(128 * 16384, 16384), k_sb[:])
                        E.dG('qPre', ydump(2 * 128 * 16384, 16384), v_sb[:])
                        E.dG('qPre', ydump(3 * 128 * 16384, 4096), lnfm[:])
                    else:
                        E.dG('qPre', ydump(0, 4096), lnfm[:])
                        E.dG('qPre', ydump(128 * 4096, 8192), q_sb[:, 0:8192])
                    E.wait('sync', 'qPre', E.c['qPre'])
                    return
                for lt in range(LT + 1):
                    scores_stage(b, lt)
                if stage == 'dbg2' and b == 0:
                    E.barrier()
                    return

            if stage in ('x1', 'ln2'):
                E.barrier()
                return

            def allreduce_scalar(acc_ap, ar_idx, outs):
                E.wait('gpsimd', 'sV', E.c['sV'])
                E.dG('qCC', d_sc, acc_ap)
                E.wait('gpsimd', 'qCC', E.c['qCC'])
                ld = E.dG('qCC', row_sb[:],
                          d_sc.rearrange("p one -> (one p)").unsqueeze(0))
                E.wait('vector', 'qCC', ld)
                E.vser(lambda e: e.reduce_max(one_sb[0:1, 0:1], row_sb[:], AX.X))
                E.wait('gpsimd', 'sV', E.c['sV'])
                E.dG('qCC', d_ar_in[ar_idx][0:1, 0:1], one_sb[0:1, 0:1])
                E.wait('gpsimd', 'qCC', E.c['qCC'])
                E.g(lambda e: e.collective_compute(
                    "AllReduce", ALU.max, replica_groups=[list(range(NCORES))],
                    ins=[d_ar_in[ar_idx][0:1, 0:1].opt()],
                    outs=[d_ar_out[ar_idx][0:1, 0:1].opt()]),
                    inc='cc', k=1)
                E.wait('gpsimd', 'cc', E.c['cc'])
                ldb = E.dG('qCC', GBUF,
                           d_ar_out[ar_idx][0:1, 0:1].broadcast_to([128, 1]))
                E.wait('vector', 'qCC', ldb)
                s_ap, r_ap, qmax = outs
                E.vser(lambda e: e.tensor_scalar(s_ap, GBUF, 1.0 / qmax, 1e-12,
                                                 ALU.mult, ALU.max))
                E.vser(lambda e: e.reciprocal(r_ap, s_ap))
                sn['scale', ar_idx] = E.c['sV']

            def allreduce_pair(acc1, acc2, ar_idx, outs1, outs2):
                for slot, acc in ((0, acc1), (1, acc2)):
                    E.wait('gpsimd', 'sV', E.c['sV'])
                    E.dG('qCC', d_sc, acc)
                    E.wait('gpsimd', 'qCC', E.c['qCC'])
                    ld = E.dG('qCC', row_sb[:],
                              d_sc.rearrange("p one -> (one p)").unsqueeze(0))
                    E.wait('vector', 'qCC', ld)
                    E.vser(lambda e: e.reduce_max(one_sb[0:1, slot:slot + 1],
                                                  row_sb[:], AX.X))
                E.wait('gpsimd', 'sV', E.c['sV'])
                E.dG('qCC', d_ar_in[ar_idx][0:1, 0:2], one_sb[0:1, 0:2])
                E.wait('gpsimd', 'qCC', E.c['qCC'])
                E.g(lambda e: e.collective_compute(
                    "AllReduce", ALU.max, replica_groups=[list(range(NCORES))],
                    ins=[d_ar_in[ar_idx].opt()], outs=[d_ar_out[ar_idx].opt()]),
                    inc='cc', k=1)
                E.wait('gpsimd', 'cc', E.c['cc'])
                for slot, (s_ap, r_ap, qmax), gb in ((0, outs1, GBUF),
                                                     (1, outs2, GBUF2)):
                    E.wait('gpsimd', 'sV', E.c['sV'])
                    ldb = E.dG('qCC', gb, d_ar_out[ar_idx][0:1, slot:slot + 1]
                               .broadcast_to([128, 1]))
                    E.wait('vector', 'qCC', ldb)
                    E.vser(lambda e: e.tensor_scalar(s_ap, gb, 1.0 / qmax, 1e-12,
                                                     ALU.mult, ALU.max))
                    E.vser(lambda e: e.reciprocal(r_ap, s_ap))
                sn['scale', ar_idx] = E.c['sV']

            allreduce_scalar(m0acc[:], 0, (S0, R0, 32767.0))
            E.barrier()

            E.dS('qW', wu_sb[:].rearrange("p (c o) -> p c o", c=HC),
                 d_wut.transpose([1, 0, 2]))
            lw = E.dS('qW', wg_sb[:].rearrange("p (c o) -> p c o", c=HC),
                      d_wgt.transpose([1, 0, 2]))
            sn['wug'] = lw

            def b_prep(g):
                for tt4 in range(4):
                    ti = g * 4 + tt4
                    par = ti % 2
                    E.wait('sync', 'sV', sn.get(('xq', ti - 2), 0))
                    ld = E.dS(f'qLN{par}', lnbuf[:, par * H:(par + 1) * H],
                              d_ln2[ti * 128:(ti + 1) * 128, :])
                    E.wait('vector', f'qLN{par}', ld)
                    E.wait('vector', 'sT', sn.get(('xqtr', ti - 2), 0))
                    E.vser(lambda e: e.tensor_scalar(tmpq[:],
                                                     lnbuf[:, par * H:(par + 1) * H],
                                                     R0, MAGIC, ALU.mult, ALU.add))
                    E.vser(lambda e: e.tensor_scalar(xqtm[:, par * H:(par + 1) * H],
                                                     tmpq[:], MAGIC, S0,
                                                     ALU.subtract, ALU.mult))
                    sn['xq', ti] = E.c['sV']
                    pp = ptrot[0] % 2
                    E.wait('tensor', 'sV', sn['xq', ti])
                    if ptrot[0] - 2 in ptfree:
                        E.wait('tensor', 'sV', ptfree[ptrot[0] - 2])
                    for kc in range(HC):
                        E.t(lambda e: e.transpose(
                            psPT[pp][:, kc * 128:(kc + 1) * 128],
                            xqtm[:, par * H + kc * 128: par * H + (kc + 1) * 128],
                            eye_sb[:]), inc=(kc == HC - 1))
                    tsn = E.c['sT']
                    E.wait('vector', 'sT', tsn)
                    E.wait('vector', 'qXQS', sn.get(('xqsp', g - 1), 0))
                    dst3 = xqfm[:, 0:HC * 512].rearrange(
                        "p (kc f) -> p kc f", f=512)[
                        :, :, tt4 * 128:(tt4 + 1) * 128]
                    E.v(lambda e: e.tensor_copy(
                        dst3, psPT[pp][:].rearrange("p (c f) -> p c f", f=128)),
                        inc=True)
                    ptfree[ptrot[0]] = E.c['sV']
                    ptrot[0] += 1
                    sn['xqtr', ti] = E.c['sT']
                sn['xqfm', g] = E.c['sV']
                E.wait('gpsimd', 'sV', sn['xqfm', g])
                sp = E.dG('qXQS', d_xq[g], xqfm[:, 0:HC * 512])
                sn['xqsp', g] = sp

            def b_mms_one(j, w_sb, dsp, acc, rhs_base, ld_sem, ld_val,
                          hook=None):
                for ic in range(IC):
                    if hook is not None:
                        hook()
                    rot = ps6rot[0]
                    pb = ps6(rot)
                    ps6_wait(rot)
                    if ic == 0:
                        E.wait('tensor', ld_sem, ld_val)
                        if j == 0:
                            E.wait('tensor', 'qW', sn['wug'])
                    for kc in range(HC):
                        E.t(lambda e: e.matmul(
                            pb,
                            w_sb[:, kc * II + ic * 128: kc * II + ic * 128 + 128],
                            xqfm[:, rhs_base + kc * 512: rhs_base + (kc + 1) * 512],
                            start=(kc == 0), stop=(kc == HC - 1)),
                            inc=(kc == HC - 1))
                    tsn = E.c['sT']
                    srot = rot % 6
                    E.wait('scalar', 'sT', tsn)
                    E.wait('scalar', f'qSG{srot}', sn.get(('bspill', rot - 6), 0))
                    E.wait('scalar', 'sV', sn.get(('bred', rot - 6), 0))
                    E.s(lambda e: e.activation(stg[:, srot * 512:(srot + 1) * 512],
                                               pb, AF.Identity,
                                               bias=0.0, scale=1.0), inc=True)
                    ps6free[rot] = ('sS', E.c['sS'])
                    ev = E.c['sS']
                    E.wait('vector', 'sS', ev)
                    E.vser(lambda e: e.reduce_max(
                        stmx[:, srot:srot + 1],
                        stg[:, srot * 512:(srot + 1) * 512],
                        AX.X, apply_absolute_value=True))
                    sn['bred', rot] = E.c['sV']
                    if srot == 5 or ic == IC - 1:
                        lo = (srot // 6) * 6
                        E.vser(lambda e: e.reduce_max(redt[:, 0:1],
                                                      stmx[:, lo:srot + 1], AX.X))
                        E.vser(lambda e: e.tensor_tensor(acc[:], acc[:],
                                                         redt[:, 0:1], ALU.max))
                        sn['bred', rot] = E.c['sV']
                    E.wait('gpsimd', 'sS', ev)
                    sp = E.dG(f'qSG{srot}', dsp[ic * 128:(ic + 1) * 128,
                                                j * 512:(j + 1) * 512],
                              stg[:, srot * 512:(srot + 1) * 512])
                    sn['bspill', rot] = sp
                    ps6rot[0] += 1

            for g in range(NGRP + 1):
                if g >= 1:
                    b_mms_one(g - 1, wg_sb, d_g, gmaxa, 0, 'sV',
                              sn.get(('xqfm', g - 1), 0))
                    sn['bmmT', g - 1] = E.c['sT']
                if g < NGRP:
                    b_prep(g)
            sn['gspills'] = {i: E.c[f'qSG{i}'] for i in range(4)}

            allreduce_scalar(gmaxa[:], 1, (SG, RG, 32767.0))
            E.vser(lambda e: e.tensor_scalar(scl[:, 12:13], SG, 1.0 / 65536.0,
                                             None, ALU.mult))
            sn['sg216'] = E.c['sV']
            SG216 = scl[:, 12:13]

            def b2_load(g):
                par = (g + 1) % 2
                if g == 0:
                    E.wait('sync', 'qXQS', sn['xqsp', 0])
                elif g == 1:
                    E.wait('sync', 'sT', sn.get(('bmmT', NGRP - 1), 0))
                    E.wait('sync', 'qXQS', sn['xqsp', NGRP - 1])
                else:
                    E.wait('sync', 'sT', sn['b2mmT', g - 2])
                ld = E.dS(f'qXQ{par}', xqfm[:, par * HC * 512:(par + 1) * HC * 512],
                          d_xq[g])
                sn['b2ld', g] = ld

            def c_tile_gen(idx):
                ic, sl = idx // NSL, idx % NSL
                par = idx % 2
                if idx == 0:
                    for i4 in range(4):
                        E.wait('sync', f'qSG{i4}', sn['gspills'][i4])
                    E.wait('vector', 'sV', sn['sg216'])
                    E.wait('scalar', 'sV', sn['sg216'])
                gb = cbuf[:, par * 2048:(par + 1) * 2048]
                E.wait('sync', 'sV', sn.get(('cgq', idx - 2), 0))
                ld = E.dS(f'qCB{par}', gb, d_g[ic * 128:(ic + 1) * 128,
                                               sl * 2048:(sl + 1) * 2048])
                E.wait('scalar', f'qCB{par}', ld)
                E.wait('scalar', 'sV', sn.get(('cgq', idx - 2), 0))
                E.sser(lambda e: e.activation(sigb[:, par * 2048:(par + 1) * 2048],
                                              gb, AF.Identity, bias=MAGIC, scale=RG))
                sn['cc1', idx] = E.c['sS']
                yield
                E.wait('vector', 'sS', sn['cc1', idx])
                E.vser(lambda e: e.tensor_scalar(
                    c1[:], sigb[:, par * 2048:(par + 1) * 2048], MAGIC, SG216,
                    ALU.subtract, ALU.mult))
                sn['cgq', idx] = E.c['sV']
                yield
                ngv = ngq[:, par * 2048:(par + 1) * 2048]
                E.wait('vector', f'qTS{par}', sn.get(('ctsp', idx - 2), 0))
                E.vser(lambda e: e.tensor_scalar(
                    ngv, sigb[:, par * 2048:(par + 1) * 2048], MAGIC, None,
                    ALU.subtract))
                sn['cngq', idx] = E.c['sV']
                E.wait('gpsimd', 'sV', sn['cngq', idx])
                spn = E.dG(f'qTS{par}',
                           d_ng[ic * 128:(ic + 1) * 128,
                                sl * 2048:(sl + 1) * 2048], ngv)
                sn['ctsp', idx] = spn
                yield
                E.wait('scalar', 'sV', sn['cgq', idx])
                E.wait('scalar', 'sV', sn.get(('ctV', idx - 1), 0))
                E.sser(lambda e: e.activation(c2[:], c1[:], AF.Sigmoid,
                                              scale=65536.0))
                yield
                E.sser(lambda e: e.activation(ngb[:], c2[:], AF.Identity,
                                              bias=MAGIC, scale=65536.0))
                sn['csig', idx] = E.c['sS']
                yield
                E.wait('vector', 'sS', sn['csig', idx])
                E.vser(lambda e: e.tensor_scalar(ngb[:], ngb[:], MAGIC, 65535.0,
                                                 ALU.subtract, ALU.min))
                sn['ckq', idx] = E.c['sV']
                yield
                tbv = tbb[:, 0:2048]
                E.vser(lambda e: e.tensor_tensor(tbv, c1[:], ngb[:], ALU.mult))
                yield
                E.vser(lambda e: e.reduce_max(redt[:, 0:1], tbv, AX.X,
                                              apply_absolute_value=True))
                E.vser(lambda e: e.tensor_tensor(tmaxa[:], tmaxa[:], redt[:, 0:1],
                                                 ALU.max))
                sn['ctV', idx] = E.c['sV']
                yield

            def all_c_gen():
                for idx in range(IC * NSL):
                    yield from c_tile_gen(idx)
            cgen = [None]

            def emit_c_tiles(n):
                if cgen[0] is None:
                    cgen[0] = all_c_gen()
                for _ in range(n):
                    next(cgen[0], None)

            for g in range(NGRP + 1):
                if g >= 1:
                    j = g - 1
                    b_mms_one(j, wu_sb, d_u, umaxa, ((j + 1) % 2) * HC * 512,
                              f'qXQ{(j + 1) % 2}', sn['b2ld', j],
                              hook=lambda: emit_c_tiles(1))
                    sn['b2mmT', j] = E.c['sT']
                if g < NGRP:
                    b2_load(g)
            emit_c_tiles(IC * NSL * 8)

            allreduce_pair(umaxa[:], tmaxa[:], 2, (SU, RU, 32767.0),
                           (ST_, RT_, 32767.0))
            E.vser(lambda e: e.tensor_tensor(STSU, ST_, SU, ALU.mult))
            E.barrier()

            NW = 4096 if NSL % 2 == 0 else 2048
            nh = max(1, (NSL * 2048) // NW)
            for idx in range(IC * nh):
                ic, s2 = idx // nh, idx % nh
                par = idx % 2
                c0 = s2 * NW
                ngv = tb2[:, par * NW:(par + 1) * NW]
                ubv = ub2[:, par * NW:(par + 1) * NW]
                bA = cb2[:, par * NW:(par + 1) * NW]
                bB = sg2[:, par * NW:(par + 1) * NW]
                bC = nt2[:, par * NW:(par + 1) * NW]
                nuv = nu2[:, par * NW:(par + 1) * NW]
                hw2 = NW // 2
                E.wait('gpsimd', 'sV', sn.get(('dtV', idx - 2), 0))
                ldng = E.dG(f'qTB{par}', ngv,
                            d_ng[ic * 128:(ic + 1) * 128, c0:c0 + NW])
                E.wait('sync', 'sS', sn.get(('duS', idx - 2), 0))
                ldua = E.dS(f'qUB{par}', ubv[:, 0:hw2],
                            d_u[ic * 128:(ic + 1) * 128, c0:c0 + hw2])
                ldub = E.dS(f'qUC{par}', ubv[:, hw2:NW],
                            d_u[ic * 128:(ic + 1) * 128, c0 + hw2:c0 + NW])
                E.wait('scalar', f'qTB{par}', ldng)
                E.wait('scalar', 'sV', sn.get(('dzv', idx - 2), 0))
                E.wait('scalar', f'qNT{par}', sn.get(('dzsp', idx - 2), 0))
                E.wait('scalar', f'qNC{par}', sn.get(('dzsp2', idx - 2), 0))
                E.sser(lambda e: e.activation(bA, ngv, AF.Sigmoid, scale=SG))
                E.sser(lambda e: e.activation(bB, bA, AF.Identity, bias=MAGIC,
                                              scale=65536.0))
                sn['dkq', idx] = E.c['sS']
                E.wait('vector', 'sS', sn['dkq', idx])
                E.vser(lambda e: e.tensor_scalar(bB, bB, MAGIC, 65535.0,
                                                 ALU.subtract, ALU.min))
                E.vser(lambda e: e.scalar_tensor_tensor(bC, ngv, SG216, bB,
                                                        ALU.mult, ALU.mult))
                sn['dtV', idx] = E.c['sV']
                E.wait('scalar', 'sV', sn['dtV', idx])
                E.sser(lambda e: e.activation(bB, bC, AF.Identity, bias=MAGIC,
                                              scale=RT_))
                sn['dtS', idx] = E.c['sS']
                E.wait('scalar', f'qUB{par}', ldua)
                E.wait('scalar', f'qUC{par}', ldub)
                E.sser(lambda e: e.activation(bA, ubv, AF.Identity, bias=MAGIC,
                                              scale=RU))
                E.sser(lambda e: e.activation(nuv, bA, AF.Identity, bias=-MAGIC,
                                              scale=1.0))
                sn['duS', idx] = E.c['sS']
                E.wait('vector', 'sS', sn['dtS', idx])
                E.vser(lambda e: e.tensor_scalar(bC, bB, MAGIC, STSU,
                                                 ALU.subtract, ALU.mult))
                E.wait('vector', 'sS', sn['duS', idx])
                E.vser(lambda e: e.tensor_tensor(bA, bC, nuv, ALU.mult))
                E.vser(lambda e: e.reduce_max(redt[:, 1:2], bA, AX.X,
                                              apply_absolute_value=True))
                E.vser(lambda e: e.tensor_tensor(zmaxa[:], zmaxa[:], redt[:, 1:2],
                                                 ALU.max))
                sn['dzv', idx] = E.c['sV']
                E.wait('gpsimd', 'sV', sn['dzv', idx])
                spza = E.dG(f'qNT{par}',
                            d_z[ic * 128:(ic + 1) * 128, c0:c0 + hw2],
                            bA[:, 0:hw2])
                spzb = E.dG(f'qNC{par}',
                            d_z[ic * 128:(ic + 1) * 128, c0 + hw2:c0 + NW],
                            bA[:, hw2:NW])
                sn['dzsp', idx] = spza
                sn['dzsp2', idx] = spzb
                
            allreduce_scalar(zmaxa[:], 3, (SZ, RZ, 32767.0))
            E.barrier()

            lw = E.dS('qW', wd_sb[:].rearrange("p (c o) -> p c o", c=IC),
                      d_wdt.transpose([1, 0, 2]))
            sn['wd'] = lw

            def e_prep_group(sl, g4):
                third = g4 // 2
                if g4 % 2 == 0:
                    hidx = sl * 3 + third
                    hpar = hidx % 2
                    r0 = third * 8 * 128
                    E.wait('gpsimd', 'sS', sn.get(('ehalf', hidx - 2), 0))
                    ldz = E.dG(f'qEN{hpar}',
                               zb_e[:, hpar * 4096:(hpar + 1) * 4096]
                               .rearrange("p (c t) -> p c t", t=512),
                               d_z[r0:r0 + 8 * 128, sl * 512:(sl + 1) * 512]
                               .rearrange("(c p) t -> p c t", p=128))
                    sn['eld', hidx] = ldz
                hidx = sl * 3 + third
                hpar = hidx % 2
                gidx = sl * 6 + g4
                gpar = gidx % 2
                inoff = hpar * 4096 + (g4 % 2) * 2048
                eos = eo1[:, gpar * 2048:(gpar + 1) * 2048]
                E.wait('scalar', f'qEN{hpar}', sn['eld', hidx])
                E.wait('scalar', 'sV', sn.get(('eo', gidx - 2), 0))
                E.sser(lambda e: e.activation(eos, zb_e[:, inoff:inoff + 2048],
                                              AF.Identity, bias=MAGIC, scale=RZ))
                sez = E.c['sS']
                if g4 % 2 == 1:
                    sn['ehalf', hidx] = sez
                E.wait('vector', 'sS', sez)
                E.wait('vector', 'sT', sn.get(('emm', sl - 2), 0))
                E.vser(lambda e: e.tensor_scalar(
                    ofm[:, (sl % 2) * IC * 512 + g4 * 2048:
                        (sl % 2) * IC * 512 + (g4 + 1) * 2048],
                    eos, MAGIC, SZ, ALU.subtract, ALU.mult))
                sn['eo', gidx] = E.c['sV']
                if g4 == 5:
                    sn['eofm', sl] = E.c['sV']

            def e_mms_t(j, tt4):
                obase = (j % 2) * IC * 512
                if True:
                    ti = j * 4 + tt4
                    parx = ti % 2
                    E.wait('sync', f'qY{parx}', sn.get(('yst', ti - 2), 0))
                    ldx = E.dS(f'qXB{parx}', x1b[:, parx * H:(parx + 1) * H],
                               d_x1[ti * 128:(ti + 1) * 128, :])
                    for hb in range(2):
                        rot = ps6rot[0]
                        pb = ps6(rot)
                        ps6_wait(rot)
                        E.wait('tensor', 'sV', sn['eofm', j])
                        if j == 0 and tt4 == 0 and hb == 0:
                            E.wait('tensor', 'qW', sn['wd'])
                        for ic in range(IC):
                            E.t(lambda e: e.matmul(
                                pb,
                                ofm[:, obase + ic * 512 + tt4 * 128:
                                    obase + ic * 512 + tt4 * 128 + 128],
                                wd_sb[:, ic * H + hb * 512: ic * H + hb * 512 + 512],
                                start=(ic == 0), stop=(ic == IC - 1)),
                                inc=(ic == IC - 1))
                        tsn = E.c['sT']
                        E.wait('vector', 'sT', tsn)
                        E.wait('vector', f'qXB{parx}', ldx)
                        E.wait('vector', f'qY{parx}', sn.get(('yst', ti - 2), 0))
                        E.v(lambda e: e.scalar_tensor_tensor(
                            y_sb[:, parx * H + hb * 512: parx * H + hb * 512 + 512],
                            pb, 1.0,
                            x1b[:, parx * H + hb * 512: parx * H + hb * 512 + 512],
                            ALU.mult, ALU.add), inc=True)
                        ps6free[rot] = ('sV', E.c['sV'])
                        ps6rot[0] += 1
                    yv = E.c['sV']
                    E.wait('sync', 'sV', yv)
                    st = E.dS(f'qY{parx}', d_y[ti * 128:(ti + 1) * 128, :],
                              y_sb[:, parx * H:(parx + 1) * H])
                    sn['yst', ti] = st
                if tt4 == 3:
                    sn['emm', j] = E.c['sT']

            for sl in range(NGRP + 1):
                steps = []
                if sl < NGRP:
                    steps += [('p', sl, 0), ('p', sl, 1)]
                if sl >= 1:
                    steps += [('m', sl - 1, 0)]
                if sl < NGRP:
                    steps += [('p', sl, 2)]
                if sl >= 1:
                    steps += [('m', sl - 1, 1)]
                if sl < NGRP:
                    steps += [('p', sl, 3)]
                if sl >= 1:
                    steps += [('m', sl - 1, 2)]
                if sl < NGRP:
                    steps += [('p', sl, 4), ('p', sl, 5)]
                if sl >= 1:
                    steps += [('m', sl - 1, 3)]
                for kind, a, b2 in steps:
                    if kind == 'p':
                        e_prep_group(a, b2)
                    else:
                        e_mms_t(a, b2)

            E.barrier()

        with nc.Block() as block:
            block.sync(lambda eng: walk(Em('sync', eng, sems)))
            block.gpsimd(lambda eng: walk(Em('gpsimd', eng, sems)))
            block.tensor(lambda eng: walk(Em('tensor', eng, sems)))
            block.vector(lambda eng: walk(Em('vector', eng, sems)))
            block.scalar(lambda eng: walk(Em('scalar', eng, sems)))

    return nc


def kernel(L0_attn_x, L0_attn_Wq, L0_attn_bq, L0_attn_Wk, L0_attn_bk,
           L0_attn_Wv, L0_attn_bv, L0_ffn_w_gate, L0_ffn_w_up, L0_ffn_w_down):
    stage = os.environ.get("ATH_STAGE", "")
    f32 = np.float32
    x = np.asarray(L0_attn_x, dtype=f32)
    wqt = np.ascontiguousarray(np.asarray(L0_attn_Wq, dtype=f32).T)
    wkt = np.ascontiguousarray(np.asarray(L0_attn_Wk, dtype=f32).T)
    wvt = np.ascontiguousarray(np.asarray(L0_attn_Wv, dtype=f32).T)
    bq2 = np.ascontiguousarray(np.asarray(L0_attn_bq, dtype=f32).reshape(HC, 128).T)
    bk2 = np.ascontiguousarray(np.asarray(L0_attn_bk, dtype=f32).reshape(HC, 128).T)
    bv_row = np.ascontiguousarray(np.asarray(L0_attn_bv, dtype=f32).reshape(1, H))
    w_up = np.ascontiguousarray(np.asarray(L0_ffn_w_up, dtype=f32))
    w_gate = np.ascontiguousarray(np.asarray(L0_ffn_w_gate, dtype=f32))
    w_down = np.ascontiguousarray(np.asarray(L0_ffn_w_down, dtype=f32))
    eye = np.eye(128, dtype=f32)
    negeye = np.asarray(np.eye(128) * NEG_BIG, dtype=f32)

    nc = build_program(stage)
    in_maps = []
    for c in range(NCORES):
        xs = np.ascontiguousarray(x[c * BPC:(c + 1) * BPC].reshape(T, H))
        in_maps.append(dict(x=xs, wqt=wqt, wkt=wkt, wvt=wvt, bq2=bq2, bk2=bk2,
                            bv_row=bv_row, w_up=w_up, w_gate=w_gate,
                            w_down=w_down, eye=eye, negeye=negeye))
    res = run_bass_kernel_spmd(nc, in_maps, core_ids=list(range(NCORES)))
    y = np.concatenate([r["y"].reshape(BPC, L, H) for r in res.results], axis=0)
    return y


# version 20
# speedup vs baseline: 1.1870x; 1.1870x over previous
import os
from contextlib import ExitStack

import numpy as np

import concourse.bass as bass
import concourse.mybir as mybir
from concourse.bass_utils import run_bass_kernel_spmd

F32 = mybir.dt.float32
BF16 = mybir.dt.bfloat16
I16 = mybir.dt.int16
I32 = mybir.dt.int32
AF = mybir.ActivationFunctionType
ALU = mybir.AluOpType
AX = mybir.AxisListType

MAGIC = float(np.float32(1.5 * 2 ** 23))
RSQRT_SEED = 0x5F3759DF
NEG_BIG = float(np.float32(-1.0e38))
EPS = 1e-5

NCORES = 8
B, L, H, II = 32, 2048, 1024, 3072
BPC = int(os.environ.get("ATH_BPC", str(B // NCORES)))
T = BPC * L
HC = H // 128
IC = II // 128
LT = L // 128
NGRP = T // 512
NSL = T // 2048

ENG_SEM = {'tensor': 'sT', 'vector': 'sV', 'scalar': 'sS'}


class Em:

    def __init__(self, which, eng, sems):
        self.which = which
        self.eng = eng
        self.sems = sems
        self.c = {k: 0 for k in sems}

    def _run(self, name, fn, inc=None, k=1):
        inst = fn(self.eng) if self.which == name else None
        if inc is not None:
            self.c[inc] += k
            if inst is not None:
                inst.then_inc(self.sems[inc], k)
        return self.c[inc] if inc else 0

    def dS(self, sem, out, in_):
        return self._run('sync', lambda e: e.dma_start(out=out, in_=in_), sem, 16)

    def dG(self, sem, out, in_):
        return self._run('gpsimd', lambda e: e.dma_start(out=out, in_=in_), sem, 16)

    def t(self, fn, inc=False):
        return self._run('tensor', fn, 'sT' if inc else None)

    def v(self, fn, inc=False):
        return self._run('vector', fn, 'sV' if inc else None)

    def s(self, fn, inc=False):
        return self._run('scalar', fn, 'sS' if inc else None)

    def g(self, fn, inc=None, k=1):
        return self._run('gpsimd', fn, inc, k)

    def wait(self, name, ctr, val):
        if self.which == name and val > 0:
            self.eng.wait_ge(self.sems[ctr], val)

    def vser(self, fn):
        self.wait('vector', 'sV', self.c['sV'])
        return self.v(fn, inc=True)

    def sser(self, fn):
        self.wait('scalar', 'sS', self.c['sS'])
        return self.s(fn, inc=True)

    def barrier(self):
        own = ENG_SEM.get(self.which)
        for ctr in self.sems:
            if ctr != own:
                self.wait(self.which, ctr, self.c[ctr])


def build_program(stage=""):
    nc = bass.Bass("TRN2", target_bir_lowering=False)

    d_x = nc.dram_tensor("x", [T, H], F32, kind="ExternalInput").ap()
    d_wqt = nc.dram_tensor("wqt", [H, H], F32, kind="ExternalInput").ap()
    d_wkt = nc.dram_tensor("wkt", [H, H], F32, kind="ExternalInput").ap()
    d_wvt = nc.dram_tensor("wvt", [H, H], F32, kind="ExternalInput").ap()
    d_bq = nc.dram_tensor("bq2", [128, HC], F32, kind="ExternalInput").ap()
    d_bk = nc.dram_tensor("bk2", [128, HC], F32, kind="ExternalInput").ap()
    d_bv = nc.dram_tensor("bv_row", [1, H], F32, kind="ExternalInput").ap()
    d_wu = nc.dram_tensor("w_up", [II, H], F32, kind="ExternalInput").ap()
    d_wg = nc.dram_tensor("w_gate", [II, H], F32, kind="ExternalInput").ap()
    d_wd = nc.dram_tensor("w_down", [H, II], F32, kind="ExternalInput").ap()
    d_eye = nc.dram_tensor("eye", [128, 128], F32, kind="ExternalInput").ap()
    d_negeye = nc.dram_tensor("negeye", [128, 128], F32, kind="ExternalInput").ap()
    d_y = nc.dram_tensor("y", [T, H], F32, kind="ExternalOutput").ap()

    d_x1 = nc.dram_tensor("x1_sp", [T, H], F32).ap()
    d_ln2 = nc.dram_tensor("ln2_sp", [T, H], F32).ap()
    d_u = nc.dram_tensor("u_sp", [II, T], F32).ap()
    d_g = nc.dram_tensor("g_sp", [II, T], F32).ap()
    d_t = nc.dram_tensor("t_sp", [II, T], F32).ap()
    d_z = nc.dram_tensor("z_sp", [II, T], F32).ap()
    d_wut = nc.dram_tensor("wut_dq", [HC, 128, II], BF16).ap()
    d_wgt = nc.dram_tensor("wgt_dq", [HC, 128, II], BF16).ap()
    d_wdt = nc.dram_tensor("wdt_dq", [IC, 128, H], BF16).ap()
    d_xq = nc.dram_tensor("xq_sp", [NGRP, 128, HC * 512], BF16).ap()
    d_sc = nc.dram_tensor("sc_rt", [128, 1], F32).ap()
    d_ar_in = [nc.dram_tensor(f"ar{i}_in", [1, 2], F32).ap() for i in range(4)]
    d_ar_out = [nc.dram_tensor(f"ar{i}_out", [1, 2], F32).ap() for i in range(4)]

    with ExitStack() as top:
        E_ = top.enter_context

        sems = {}
        sem_names = ['sT', 'sV', 'sS', 'sG', 'cc', 'qPre', 'qWQ', 'qW', 'qCC',
                     'qWN0', 'qWN1', 'qST0', 'qST1', 'qX0', 'qX1', 'qX2', 'qX3',
                     'qSP0', 'qSP1', 'qLN0', 'qLN1',
                     'qSG0', 'qSG1', 'qSG2', 'qSG3', 'qSG4', 'qSG5',
                     'qCB0', 'qCB1', 'qTS0', 'qTS1',
                     'qTB0', 'qTB1', 'qUB0', 'qUB1',
                     'qTC0', 'qTC1', 'qUC0', 'qUC1', 'qNC0', 'qNC1',
                     'qNT0', 'qNT1', 'qNU0', 'qNU1',
                     'qEN0', 'qEN1', 'qXB0', 'qXB1', 'qY0', 'qY1', 'qXQ0', 'qXQ1', 'qXQS']
        for name in sem_names:
            sems[name] = E_(nc.semaphore(name))

        psS = E_(nc.psum_tensor("psS", [128, 2048], F32))
        psA = E_(nc.psum_tensor("psA", [128, 1024], F32))
        psPT = [E_(nc.psum_tensor(f"psPT{i}", [128, 1024], BF16)) for i in range(2)]

        eye_sb = E_(nc.sbuf_tensor("eye_sb", [128, 128], BF16))
        negeye_sb = E_(nc.sbuf_tensor("negeye_sb", [128, 128], F32))
        bq_sb = E_(nc.sbuf_tensor("bq_sb", [128, HC], F32))
        bk_sb = E_(nc.sbuf_tensor("bk_sb", [128, HC], F32))
        bv_sb = E_(nc.sbuf_tensor("bv_sb", [128, H], BF16))
        cseed = E_(nc.sbuf_tensor("cseed", [128, 4], I32))
        cmagic = E_(nc.sbuf_tensor("cmagic", [128, 4], F32))
        nc.const_aps.register(F32, MAGIC, cmagic[:, 0:1])
        nc.const_aps.register(F32, 0.0, cmagic[:, 1:2])
        nc.const_aps.register(F32, EPS, cmagic[:, 2:3])
        nc.const_aps.register(F32, -MAGIC, cmagic[:, 3:4])
        scl = E_(nc.sbuf_tensor("scl", [128, 16], F32))
        stw = E_(nc.sbuf_tensor("stw", [128, 64], F32))
        row_sb = E_(nc.sbuf_tensor("row_sb", [1, 128], F32))
        one_sb = E_(nc.sbuf_tensor("one_sb", [1, 4], F32))
        m0acc = E_(nc.sbuf_tensor("m0acc", [128, 1], F32))
        umaxa = E_(nc.sbuf_tensor("umaxa", [128, 1], F32))
        gmaxa = E_(nc.sbuf_tensor("gmaxa", [128, 1], F32))
        tmaxa = E_(nc.sbuf_tensor("tmaxa", [128, 1], F32))
        zmaxa = E_(nc.sbuf_tensor("zmaxa", [128, 1], F32))
        redt = E_(nc.sbuf_tensor("redt", [128, 2], F32))

        att = ExitStack()
        A_ = att.enter_context
        wq_sb = A_(nc.sbuf_tensor("wq_sb", [128, HC * H], BF16))
        wk_sb = A_(nc.sbuf_tensor("wk_sb", [128, HC * H], BF16))
        wv_sb = A_(nc.sbuf_tensor("wv_sb", [128, HC * H], BF16))
        q_sb = A_(nc.sbuf_tensor("q_sb", [128, HC * L], BF16))
        k_sb = A_(nc.sbuf_tensor("k_sb", [128, HC * L], BF16))
        v_sb = A_(nc.sbuf_tensor("v_sb", [128, LT * H], BF16))
        lnfm = A_(nc.sbuf_tensor("lnfm", [128, HC * 512], BF16))
        P_sb = A_(nc.sbuf_tensor("P_sb", [128, 2 * 2048], BF16))
        pt_sb = A_(nc.sbuf_tensor("pt_sb", [128, 2048], BF16))
        x1_sb = A_(nc.sbuf_tensor("x1_sb", [128, 2 * H], F32))
        ln2_sb = A_(nc.sbuf_tensor("ln2_sb", [128, 2 * H], F32))
        xbuf = A_(nc.sbuf_tensor("xbuf", [128, 4 * H], F32))
        ln1tm = A_(nc.sbuf_tensor("ln1tm", [128, 2 * H], BF16))
        sqdead = A_(nc.sbuf_tensor("sqdead", [128, H], BF16))
        rmaxb = A_(nc.sbuf_tensor("rmaxb", [128, 2], F32))
        negb = A_(nc.sbuf_tensor("negb", [128, 2], F32))
        rsacc = A_(nc.sbuf_tensor("rsacc", [128, 2], F32))
        recipb = A_(nc.sbuf_tensor("recipb", [128, 2], F32))
        att.close()

        deq = ExitStack()
        D_ = deq.enter_context
        wnat = D_(nc.sbuf_tensor("wnat", [128, 2 * II], F32))
        dq1 = D_(nc.sbuf_tensor("dq1", [128, II], F32))
        dqr = D_(nc.sbuf_tensor("dqr", [128, II], F32))
        dqo = D_(nc.sbuf_tensor("dqo", [128, II], BF16))
        sblk = D_(nc.sbuf_tensor("sblk", [128, II // 32], F32))
        rblk = D_(nc.sbuf_tensor("rblk", [128, II // 32], F32))
        wstg = D_(nc.sbuf_tensor("wstg", [128, IC * H], BF16))
        deq.close()

        ffn = ExitStack()
        F_ = ffn.enter_context
        wu_sb = F_(nc.sbuf_tensor("wu_sb", [128, HC * II], BF16, side='right'))
        wg_sb = F_(nc.sbuf_tensor("wg_sb", [128, HC * II], BF16, side='right'))
        xqfm = F_(nc.sbuf_tensor("xqfm", [128, 2 * HC * 512], BF16, side='right'))
        stg = F_(nc.sbuf_tensor("stg", [128, 6 * 512], F32, side='right'))
        stmx = F_(nc.sbuf_tensor("stmx", [128, 6], F32, side='right'))
        b1t = ExitStack()
        B1_ = b1t.enter_context
        lnbuf = B1_(nc.sbuf_tensor("lnbuf", [128, 2 * H], F32))
        tmpq = B1_(nc.sbuf_tensor("tmpq", [128, H], F32))
        xqtm = B1_(nc.sbuf_tensor("xqtm", [128, 2 * H], BF16))
        b1t.close()
        cd = ExitStack()
        C_ = cd.enter_context
        cbuf = C_(nc.sbuf_tensor("cbuf", [128, 2 * 2048], F32))
        c1 = C_(nc.sbuf_tensor("c1", [128, 2048], F32))
        c2 = C_(nc.sbuf_tensor("c2", [128, 2048], F32))
        ngb = C_(nc.sbuf_tensor("ngb", [128, 2048], F32))
        sigb = C_(nc.sbuf_tensor("sigb", [128, 2 * 2048], F32))
        tbb = C_(nc.sbuf_tensor("tbb", [128, 2 * 2048], F32))
        ffn.close()
        cd.close()
        dsc = ExitStack()
        D_2 = dsc.enter_context
        tb2 = D_2(nc.sbuf_tensor("tb2", [128, 2 * 4096], F32))
        ub2 = D_2(nc.sbuf_tensor("ub2", [128, 2 * 4096], F32))
        cb2 = D_2(nc.sbuf_tensor("cb2", [128, 2 * 4096], F32))
        sg2 = D_2(nc.sbuf_tensor("sg2", [128, 2 * 4096], F32))
        nt2 = D_2(nc.sbuf_tensor("nt2", [128, 2 * 4096], F32))
        nu2 = D_2(nc.sbuf_tensor("nu2", [128, 2 * 4096], F32))
        dsc.close()

        eph = ExitStack()
        P2_ = eph.enter_context
        wd_sb = P2_(nc.sbuf_tensor("wd_sb", [128, IC * H], BF16))
        ofm = P2_(nc.sbuf_tensor("ofm", [128, 2 * IC * 512], BF16))
        zb_e = P2_(nc.sbuf_tensor("zb_e", [128, 2 * 4096], F32))
        eo1 = P2_(nc.sbuf_tensor("eo1", [128, 2 * 2048], F32))
        x1b = P2_(nc.sbuf_tensor("x1b", [128, 2 * H], F32))
        y_sb = P2_(nc.sbuf_tensor("y_sb", [128, 2 * H], F32))
        eph.close()

        S0, R0 = scl[:, 0:1], scl[:, 1:2]
        SU, RU = scl[:, 2:3], scl[:, 3:4]
        SG, RG = scl[:, 4:5], scl[:, 5:6]
        ST_, RT_ = scl[:, 6:7], scl[:, 7:8]
        SZ, RZ = scl[:, 8:9], scl[:, 9:10]
        GBUF, GBUF2 = scl[:, 10:11], scl[:, 11:12]
        STSU = scl[:, 13:14]

        def walk(E: Em):
            sn = {}

            def ydump(off, c):
                return d_y.rearrange("a h -> (a h)")[off:off + 128 * c] \
                    .rearrange("(p c) -> p c", p=128)

            E.dS('qW', bq_sb[:], d_bq)
            E.dS('qW', bk_sb[:], d_bk)
            E.dS('qW', negeye_sb[:], d_negeye)
            E.dG('qPre', eye_sb[:], d_eye)
            E.dG('qPre', bv_sb[:], d_bv.broadcast_to([128, H]))
            E.v(lambda e: e.memset(cseed[:], RSQRT_SEED))
            E.v(lambda e: e.memset(cmagic[:, 0:1], MAGIC))
            E.v(lambda e: e.memset(cmagic[:, 1:2], 0.0))
            E.v(lambda e: e.memset(cmagic[:, 2:3], EPS))
            E.v(lambda e: e.memset(cmagic[:, 3:4], -MAGIC))
            E.v(lambda e: e.memset(m0acc[:], 0.0))
            E.v(lambda e: e.memset(umaxa[:], 0.0))
            E.v(lambda e: e.memset(gmaxa[:], 0.0))
            E.v(lambda e: e.memset(tmaxa[:], 0.0))
            E.v(lambda e: e.memset(zmaxa[:], 0.0), inc=True)
            E.barrier()

            def dequant_weight(dw_nat, dw_out, O, K):
                OC, KC, KB = O // 128, K // 128, K // 32
                stq = sn.get('wstg_stored', 0)
                for oc in range(OC):
                    par = oc % 2
                    E.wait('sync', 'sV', sn.get(('dqfree', par), 0))
                    ld = E.dS(f'qWN{par}', wnat[:, par * II:par * II + K],
                              dw_nat[oc * 128:(oc + 1) * 128, :])
                    wn = wnat[:, par * II:par * II + K]
                    wn3 = wn.rearrange("p (nb b) -> p nb b", b=32)
                    E.wait('vector', f'qWN{par}', ld)
                    E.vser(lambda e: e.reduce_max(sblk[:, 0:KB], wn3, AX.X,
                                                  apply_absolute_value=True))
                    E.vser(lambda e: e.tensor_scalar(sblk[:, 0:KB], sblk[:, 0:KB],
                                                     1.0 / 7.0, 1e-12, ALU.mult,
                                                     ALU.max))
                    E.vser(lambda e: e.reciprocal(rblk[:, 0:KB], sblk[:, 0:KB]))
                    rb3 = rblk[:, 0:KB].rearrange("p nb -> p nb ()") \
                        .broadcast_to([128, KB, 32])
                    sb3 = sblk[:, 0:KB].rearrange("p nb -> p nb ()") \
                        .broadcast_to([128, KB, 32])
                    E.vser(lambda e: e.tensor_tensor(
                        dq1[:, 0:K].rearrange("p (nb b) -> p nb b", b=32), wn3, rb3,
                        ALU.mult))
                    sn['dqfree', par] = E.c['sV']
                    dq1v = E.c['sV']
                    E.wait('scalar', 'sV', dq1v)
                    E.sser(lambda e: e.activation(dqr[:, 0:K], dq1[:, 0:K],
                                                  AF.Identity, bias=MAGIC,
                                                  scale=1.0))
                    r1s = E.c['sS']
                    E.wait('vector', 'sS', r1s)
                    E.vser(lambda e: e.scalar_tensor_tensor(
                        dqo[:, 0:K].rearrange("p (nb b) -> p nb b", b=32),
                        dqr[:, 0:K].rearrange("p (nb b) -> p nb b", b=32), MAGIC,
                        sb3, ALU.subtract, ALU.mult))
                    dq_ready = E.c['sV']
                    wst3 = wstg[:, 0:KC * O].rearrange("p (kc o) -> p kc o", o=O)
                    for rnd in range(KC // 8):
                        pp = rnd % 2
                        E.wait('tensor', 'sS', sn.get(('ptev', pp), 0))
                        E.wait('tensor', 'sV', dq_ready)
                        for s in range(8):
                            kc = rnd * 8 + s
                            E.t(lambda e: e.transpose(
                                psPT[pp][:, s * 128:(s + 1) * 128],
                                dqo[:, kc * 128:(kc + 1) * 128],
                                eye_sb[:]), inc=(s == 7))
                        tsn = E.c['sT']
                        E.wait('scalar', 'sT', tsn)
                        if oc == 0 and rnd == 0:
                            E.wait('scalar', 'qST0', stq)
                        E.sser(lambda e: e.activation(
                            wst3[:, rnd * 8:(rnd + 1) * 8, oc * 128:(oc + 1) * 128],
                            psPT[pp][:].rearrange("p (c f) -> p c f", f=128),
                            AF.Identity, bias=0.0, scale=1.0))
                        sn['ptev', pp] = E.c['sS']
                ev_all = E.c['sS']
                E.wait('gpsimd', 'sS', ev_all)
                for kc in range(KC):
                    st = E.dG('qST0', dw_out[kc, :, :], wstg[:, kc * O:(kc + 1) * O])
                sn['wstg_stored'] = E.c['qST0']

            dequant_weight(d_wu, d_wut, II, H)
            dequant_weight(d_wg, d_wgt, II, H)
            dequant_weight(d_wd, d_wdt, H, II)
            E.barrier()

            w2rot = [0]
            w2free = {}
            ptrot = [0]
            ptfree = {}

            def psa_wait(rot):
                ent = w2free.get(rot - 2)
                if ent is not None:
                    E.wait('tensor', ent[0], ent[1])

            ps6rot = [0]
            ps6free = {}

            def ps6(rot):
                k = rot % 6
                if k < 2:
                    return psA[:, k * 512:(k + 1) * 512]
                return psS[:, (k - 2) * 512:(k - 1) * 512]

            def ps6_wait(rot):
                ent = ps6free.get(rot - 6)
                if ent is not None:
                    E.wait('tensor', ent[0], ent[1])

            def qkv_group(b, g):
                for tt4 in range(4):
                    i = g * 4 + tt4
                    gi = b * 16 + i
                    slot = tt4
                    if g >= 1:
                        E.wait('sync', 'sS', sn.get(('ln1', b, i - 4), 0))
                    elif b > 0:
                        E.wait('sync', 'sV', sn.get(('x1ev', b - 1, 12 + tt4), 0))
                    ld = E.dS(f'qX{slot}', xbuf[:, slot * H:(slot + 1) * H],
                              d_x[gi * 128:(gi + 1) * 128, :])
                    sn['ldx', b, i] = ld
                    E.wait('vector', f'qX{slot}', ld)
                    E.v(lambda e: e.reduce_sum(stw[:, 32 + tt4:33 + tt4],
                                               xbuf[:, slot * H:(slot + 1) * H],
                                               AX.X), inc=True)
                    E.wait('scalar', f'qX{slot}', ld)
                    if tt4 == 0:
                        E.wait('scalar', 'sV', sn.get('ch1prev', 0))
                    E.sser(lambda e: e.activation(sqdead[:],
                                                  xbuf[:, slot * H:(slot + 1) * H],
                                                  AF.Square,
                                                  accum_out=stw[:, 36 + tt4:37 + tt4]))
                sn['sq4', b, g] = E.c['sS']
                sn['rs4', b, g] = E.c['sV']
                m2 = stw[:, 40:44]
                esq = stw[:, 44:48]
                var = stw[:, 48:52]
                rstd = stw[:, 52:56]
                negmrs = stw[:, 56:60]
                lnv = stw[:, 60:64]
                E.wait('scalar', 'sV', sn['rs4', b, g])
                E.sser(lambda e: e.activation(m2, stw[:, 32:36], AF.Square,
                                              scale=1.0 / H))
                E.sser(lambda e: e.activation(esq, stw[:, 36:40], AF.Identity,
                                              bias=0.0, scale=1.0 / H))
                sms = E.c['sS']
                E.wait('vector', 'sS', sms)
                E.vser(lambda e: e.tensor_tensor(var, esq, m2, ALU.subtract))
                svv = E.c['sV']
                E.wait('scalar', 'sV', svv)
                E.sser(lambda e: e.activation(lnv, var, AF.Ln, bias=EPS))
                E.sser(lambda e: e.activation(rstd, lnv, AF.Exp, scale=-0.5))
                srs = E.c['sS']
                E.wait('vector', 'sS', srs)
                E.vser(lambda e: e.scalar_tensor_tensor(
                    negmrs, stw[:, 32:36], -1.0 / H, rstd, ALU.mult, ALU.mult))
                sn['ch1', b, g] = E.c['sV']
                sn['ch1prev'] = E.c['sV']
                for tt4 in range(4):
                    i = g * 4 + tt4
                    par = i % 2
                    slot = tt4
                    E.wait('scalar', 'sV', sn['ch1', b, g])
                    E.wait('scalar', 'sT', sn.get(('ln1tr', b, i - 2), 0))
                    E.s(lambda e: e.activation(ln1tm[:, par * H:(par + 1) * H],
                                               xbuf[:, slot * H:(slot + 1) * H],
                                               AF.Identity,
                                               bias=negmrs[:, tt4:tt4 + 1],
                                               scale=rstd[:, tt4:tt4 + 1]), inc=True)
                    sn['ln1', b, i] = E.c['sS']
                    pp = ptrot[0] % 2
                    E.wait('tensor', 'sS', sn['ln1', b, i])
                    if ptrot[0] - 2 in ptfree:
                        E.wait('tensor', 'sV', ptfree[ptrot[0] - 2])
                    for kc in range(HC):
                        E.t(lambda e: e.transpose(
                            psPT[pp][:, kc * 128:(kc + 1) * 128],
                            ln1tm[:, par * H + kc * 128: par * H + (kc + 1) * 128],
                            eye_sb[:]), inc=(kc == HC - 1))
                    tsn = E.c['sT']
                    E.wait('vector', 'sT', tsn)
                    dst3 = lnfm.rearrange("p (kc f) -> p kc f", f=512)[
                        :, :, tt4 * 128:(tt4 + 1) * 128]
                    E.v(lambda e: e.tensor_copy(
                        dst3, psPT[pp][:].rearrange("p (c f) -> p c f", f=128)),
                        inc=True)
                    ptfree[ptrot[0]] = E.c['sV']
                    ptrot[0] += 1
                    sn['ln1tr', b, i] = E.c['sT']
                sn['lnfm', b, g] = E.c['sV']

            def qkv_mms(b, j):
                gbase = 0
                if j == 0:
                    E.wait('tensor', 'qWQ', sn['wqkv', b])
                for (w_sb, bias_sb, dst_sb) in ((wq_sb, bq_sb, q_sb),
                                                (wk_sb, bk_sb, k_sb)):
                    for oc in range(HC):
                        rot = w2rot[0]
                        half = (rot % 2) * 512
                        psa_wait(rot)
                        E.wait('tensor', 'sV', sn['lnfm', b, j])
                        for kc in range(HC):
                            E.t(lambda e: e.matmul(
                                psA[:, half:half + 512],
                                w_sb[:, kc * H + oc * 128: kc * H + oc * 128 + 128],
                                lnfm[:, gbase + kc * 512: gbase + (kc + 1) * 512],
                                start=(kc == 0), stop=(kc == HC - 1)),
                                inc=(kc == HC - 1))
                        tsn = E.c['sT']
                        E.wait('scalar', 'sT', tsn)
                        col = oc * L + j * 512
                        E.s(lambda e: e.activation(dst_sb[:, col:col + 512],
                                                   psA[:, half:half + 512], AF.Identity,
                                                   bias=bias_sb[:, oc:oc + 1],
                                                   scale=1.0), inc=True)
                        w2free[rot] = ('sS', E.c['sS'])
                        w2rot[0] += 1
                for tt4 in range(4):
                    for hb in range(2):
                        rot = w2rot[0]
                        half = (rot % 2) * 512
                        psa_wait(rot)
                        for kc in range(HC):
                            E.t(lambda e: e.matmul(
                                psA[:, half:half + 512],
                                lnfm[:, gbase + kc * 512 + tt4 * 128:
                                     gbase + kc * 512 + tt4 * 128 + 128],
                                wv_sb[:, kc * H + hb * 512: kc * H + hb * 512 + 512],
                                start=(kc == 0), stop=(kc == HC - 1)),
                                inc=(kc == HC - 1))
                        tsn = E.c['sT']
                        E.wait('vector', 'sT', tsn)
                        mt = j * 4 + tt4
                        E.v(lambda e: e.scalar_tensor_tensor(
                            v_sb[:, mt * H + hb * 512: mt * H + hb * 512 + 512],
                            psA[:, half:half + 512], 1.0,
                            bv_sb[:, hb * 512:hb * 512 + 512],
                            ALU.mult, ALU.add), inc=True)
                        w2free[rot] = ('sV', E.c['sV'])
                        w2rot[0] += 1
                sn['mmT', b, j] = E.c['sT']

            def smax_exp(b, lt):
                par = lt % 2
                E.wait('vector', 'sT', sn['sc', b, lt])
                E.wait('vector', 'sS', sn.get(('exp', b, lt - 2), 0))
                dcol = lt * 128
                E.vser(lambda e: e.tensor_tensor(psS[:, dcol:dcol + 128],
                                                 psS[:, dcol:dcol + 128],
                                                 negeye_sb[:], ALU.add))
                E.vser(lambda e: e.reduce_max(rmaxb[:, par:par + 1], psS[:],
                                              AX.X))
                E.vser(lambda e: e.tensor_scalar(negb[:, par:par + 1],
                                                 rmaxb[:, par:par + 1],
                                                 -1.0 / 1024.0, None, ALU.mult))
                sn['smax', b, lt] = E.c['sV']
                E.wait('scalar', 'sV', sn['smax', b, lt])
                ptprev = sn.get(('pt', b, lt - 2), sn.get(('pt', b - 1, 14 + lt), 0))
                E.wait('scalar', 'sT', ptprev)
                E.s(lambda e: e.activation(P_sb[:, par * 2048:(par + 1) * 2048],
                                           psS[:], AF.Exp,
                                           bias=negb[:, par:par + 1],
                                           scale=1.0 / 1024.0,
                                           accum_out=rsacc[:, par:par + 1]),
                    inc=True)
                sn['exp', b, lt] = E.c['sS']

            def scores_stage(b, lt):
                if lt < LT:
                    gi = b * 16 + lt
                    slot = lt % 4
                    if lt >= 4:
                        E.wait('sync', 'sV', sn.get(('x1ev', b, lt - 4), 0))
                    else:
                        E.wait('sync', 'sS', sn.get(('ln1', b, 12 + lt), 0))
                    ld = E.dS(f'qX{slot}', xbuf[:, slot * H:(slot + 1) * H],
                              d_x[gi * 128:(gi + 1) * 128, :])
                    sn['ldxr', b, lt] = ld
                    E.wait('tensor', 'sS', sn.get(('exp', b, lt - 1), 0))
                    E.wait('tensor', 'sV', sn.get(('smax', b, lt - 1), 0))
                    if lt == 0:
                        E.wait('tensor', 'sV', sn['qkvdoneV', b])
                        E.wait('tensor', 'sS', sn['qkvdoneS', b])
                    for oc in range(HC):
                        for nb in range(4):
                            E.t(lambda e: e.matmul(
                                psS[:, nb * 512:(nb + 1) * 512],
                                q_sb[:, oc * L + lt * 128: oc * L + lt * 128 + 128],
                                k_sb[:, oc * L + nb * 512: oc * L + (nb + 1) * 512],
                                start=(oc == 0), stop=(oc == HC - 1)),
                                inc=(oc == HC - 1 and nb == 3))
                    sn['sc', b, lt] = E.c['sT']
                    if lt == 0:
                        smax_exp(b, 0)
                if lt >= 1:
                    j = lt - 1
                    jp = j % 2
                    for rnd in range(2):
                        pp = ptrot[0] % 2
                        E.wait('tensor', 'sS', sn['exp', b, j])
                        if ptrot[0] - 2 in ptfree:
                            E.wait('tensor', 'sV', ptfree[ptrot[0] - 2])
                        for s in range(8):
                            mt = rnd * 8 + s
                            E.t(lambda e: e.transpose(
                                psPT[pp][:, s * 128:(s + 1) * 128],
                                P_sb[:, jp * 2048 + mt * 128:
                                     jp * 2048 + (mt + 1) * 128],
                                eye_sb[:]), inc=(s == 7))
                        tsn = E.c['sT']
                        E.wait('vector', 'sT', tsn)
                        avprev = sn.get(('av', b, j - 1),
                                        sn.get(('av', b - 1, LT - 1), 0))
                        E.wait('vector', 'sT', avprev)
                        E.v(lambda e: e.tensor_copy(
                            pt_sb[:, rnd * 1024:(rnd + 1) * 1024],
                            psPT[pp][:]), inc=True)
                        ptfree[ptrot[0]] = E.c['sV']
                        ptrot[0] += 1
                    sn['pt', b, j] = E.c['sT']
                    ptall = E.c['sV']
                    E.wait('vector', 'sS', sn['exp', b, j])
                    E.v(lambda e: e.reciprocal(recipb[:, jp:jp + 1],
                                               rsacc[:, jp:jp + 1]), inc=True)
                    sn['recip', b, j] = E.c['sV']
                    if lt < LT:
                        smax_exp(b, lt)
                    E.wait('tensor', 'sV', ptall)
                    E.wait('tensor', 'sV', sn.get(('x1ev', b, j - 1),
                                                  sn.get(('x1ev', b - 1, LT - 1), 0)))
                    for hb in range(2):
                        for mt in range(LT):
                            E.t(lambda e: e.matmul(
                                psA[:, hb * 512:(hb + 1) * 512],
                                pt_sb[:, mt * 128:(mt + 1) * 128],
                                v_sb[:, mt * H + hb * 512: mt * H + hb * 512 + 512],
                                start=(mt == 0), stop=(mt == LT - 1)),
                                inc=(mt == LT - 1 and hb == 1))
                    sn['av', b, j] = E.c['sT']
                    jslot = j % 4
                    E.wait('vector', 'sT', sn['av', b, j])
                    E.wait('vector', 'sV', sn['recip', b, j])
                    E.wait('vector', f'qX{jslot}', sn['ldxr', b, j])
                    E.wait('vector', f'qSP{jp}', sn.get(('spill', b, j - 2), 0))
                    for hb in range(2):
                        E.v(lambda e: e.scalar_tensor_tensor(
                            x1_sb[:, jp * H + hb * 512: jp * H + hb * 512 + 512],
                            psA[:, hb * 512:(hb + 1) * 512],
                            recipb[:, jp:jp + 1],
                            xbuf[:, jslot * H + hb * 512: jslot * H + hb * 512 + 512],
                            ALU.mult, ALU.add), inc=(hb == 1))
                    sn['x1ev', b, j] = E.c['sV']
                    x1t = x1_sb[:, jp * H:(jp + 1) * H]
                    E.wait('vector', 'sV', sn['x1ev', b, j])
                    E.v(lambda e: e.reduce_sum(stw[:, 16:17], x1t, AX.X), inc=True)
                    E.wait('scalar', 'sV', sn['x1ev', b, j])
                    E.sser(lambda e: e.activation(sqdead[:], x1t, AF.Square,
                                                  accum_out=stw[:, 17:18]))
                    sn['sq2', b, j] = E.c['sS']
                    sn['rs2', b, j] = E.c['sV']
                    m22, esq2, var2 = stw[:, 18:19], stw[:, 19:20], stw[:, 20:21]
                    lnv2, rstd2, negm2 = stw[:, 21:22], stw[:, 22:23], stw[:, 23:24]
                    E.wait('scalar', 'sV', sn['rs2', b, j])
                    E.sser(lambda e: e.activation(m22, stw[:, 16:17], AF.Square,
                                                  scale=1.0 / H))
                    E.sser(lambda e: e.activation(esq2, stw[:, 17:18], AF.Identity,
                                                  bias=0.0, scale=1.0 / H))
                    sms2 = E.c['sS']
                    E.wait('vector', 'sS', sms2)
                    E.vser(lambda e: e.tensor_tensor(var2, esq2, m22, ALU.subtract))
                    svv2 = E.c['sV']
                    E.wait('scalar', 'sV', svv2)
                    E.sser(lambda e: e.activation(lnv2, var2, AF.Ln, bias=EPS))
                    E.sser(lambda e: e.activation(rstd2, lnv2, AF.Exp, scale=-0.5))
                    srs2 = E.c['sS']
                    E.wait('vector', 'sS', srs2)
                    E.vser(lambda e: e.scalar_tensor_tensor(
                        negm2, stw[:, 16:17], -1.0 / H, rstd2, ALU.mult, ALU.mult))
                    sn['ch2', b, j] = E.c['sV']
                    E.wait('scalar', 'sV', sn['ch2', b, j])
                    E.wait('scalar', f'qSP{jp}', sn.get(('spill', b, j - 2), 0))
                    E.s(lambda e: e.activation(ln2_sb[:, jp * H:(jp + 1) * H], x1t,
                                               AF.Identity, bias=negm2, scale=rstd2),
                        inc=True)
                    sn['ln2', b, j] = E.c['sS']
                    E.wait('vector', 'sS', sn['ln2', b, j])
                    E.vser(lambda e: e.reduce_max(stw[:, 28:29],
                                                  ln2_sb[:, jp * H:(jp + 1) * H],
                                                  AX.X, apply_absolute_value=True))
                    E.vser(lambda e: e.tensor_tensor(m0acc[:], m0acc[:],
                                                     stw[:, 28:29], ALU.max))
                    gi = b * 16 + j
                    E.wait('gpsimd', 'sV', sn['x1ev', b, j])
                    E.dG(f'qSP{jp}', d_x1[gi * 128:(gi + 1) * 128, :], x1t)
                    E.wait('gpsimd', 'sS', sn['ln2', b, j])
                    sp = E.dG(f'qSP{jp}', d_ln2[gi * 128:(gi + 1) * 128, :],
                              ln2_sb[:, jp * H:(jp + 1) * H])
                    sn['spill', b, j] = sp
                    if stage == 'dbg2' and b == 0 and j == 0:
                        E.wait('gpsimd', 'sS', E.c['sS'])
                        E.wait('gpsimd', 'sV', E.c['sV'])
                        off = [0]

                        def nxt(c):
                            o = off[0]
                            off[0] += 128 * c
                            return ydump(o, c)
                        E.dG('qPre', nxt(2048), P_sb[:, 0:2048])
                        E.dG('qPre', nxt(2048), pt_sb[:])
                        E.dG('qPre', nxt(32), stw[:, 0:32])
                        E.dG('qPre', nxt(2), rmaxb[:])
                        E.dG('qPre', nxt(2), rsacc[:])
                        E.dG('qPre', nxt(2), recipb[:])
                        E.dG('qPre', nxt(2048), x1_sb[:])
                    if stage in ('x1', 'ln2'):
                        if stage == 'x1':
                            E.wait('sync', 'sV', sn['x1ev', b, j])
                            E.dS('qY0', d_y[gi * 128:(gi + 1) * 128, :], x1t)
                        else:
                            E.wait('sync', 'sS', sn['ln2', b, j])
                            E.dS('qY0', d_y[gi * 128:(gi + 1) * 128, :],
                                 ln2_sb[:, jp * H:(jp + 1) * H])

            for b in range(BPC):
                if b == 0:
                    for w_sb, dsrc in ((wq_sb, d_wqt), (wk_sb, d_wkt),
                                       (wv_sb, d_wvt)):
                        lw = E.dG('qWQ',
                                  w_sb[:].rearrange("p (c o) -> p c o", c=HC),
                                  dsrc.rearrange("(c p) o -> p c o", p=128))
                sn['wqkv', b] = lw
                for g in range(5):
                    if g >= 1:
                        qkv_mms(b, g - 1)
                    if g < 4:
                        qkv_group(b, g)
                sn['qkvdoneS', b] = E.c['sS']
                sn['qkvdoneV', b] = E.c['sV']
                if stage == 'dbg1' and b == 0:
                    E.wait('gpsimd', 'sS', E.c['sS'])
                    E.wait('gpsimd', 'sV', E.c['sV'])
                    if T * H >= 4 * 128 * 16384:
                        E.dG('qPre', ydump(0, 16384), q_sb[:])
                        E.dG('qPre', ydump(128 * 16384, 16384), k_sb[:])
                        E.dG('qPre', ydump(2 * 128 * 16384, 16384), v_sb[:])
                        E.dG('qPre', ydump(3 * 128 * 16384, 4096), lnfm[:])
                    else:
                        E.dG('qPre', ydump(0, 4096), lnfm[:])
                        E.dG('qPre', ydump(128 * 4096, 8192), q_sb[:, 0:8192])
                    E.wait('sync', 'qPre', E.c['qPre'])
                    return
                for lt in range(LT + 1):
                    scores_stage(b, lt)
                if stage == 'dbg2' and b == 0:
                    E.barrier()
                    return

            if stage in ('x1', 'ln2'):
                E.barrier()
                return

            def allreduce_scalar(acc_ap, ar_idx, outs):
                E.wait('gpsimd', 'sV', E.c['sV'])
                E.dG('qCC', d_sc, acc_ap)
                E.wait('gpsimd', 'qCC', E.c['qCC'])
                ld = E.dG('qCC', row_sb[:],
                          d_sc.rearrange("p one -> (one p)").unsqueeze(0))
                E.wait('vector', 'qCC', ld)
                E.vser(lambda e: e.reduce_max(one_sb[0:1, 0:1], row_sb[:], AX.X))
                E.wait('gpsimd', 'sV', E.c['sV'])
                E.dG('qCC', d_ar_in[ar_idx][0:1, 0:1], one_sb[0:1, 0:1])
                E.wait('gpsimd', 'qCC', E.c['qCC'])
                E.g(lambda e: e.collective_compute(
                    "AllReduce", ALU.max, replica_groups=[list(range(NCORES))],
                    ins=[d_ar_in[ar_idx][0:1, 0:1].opt()],
                    outs=[d_ar_out[ar_idx][0:1, 0:1].opt()]),
                    inc='cc', k=1)
                E.wait('gpsimd', 'cc', E.c['cc'])
                ldb = E.dG('qCC', GBUF,
                           d_ar_out[ar_idx][0:1, 0:1].broadcast_to([128, 1]))
                E.wait('vector', 'qCC', ldb)
                s_ap, r_ap, qmax = outs
                E.vser(lambda e: e.tensor_scalar(s_ap, GBUF, 1.0 / qmax, 1e-12,
                                                 ALU.mult, ALU.max))
                E.vser(lambda e: e.reciprocal(r_ap, s_ap))
                sn['scale', ar_idx] = E.c['sV']

            def allreduce_pair(acc1, acc2, ar_idx, outs1, outs2):
                for slot, acc in ((0, acc1), (1, acc2)):
                    E.wait('gpsimd', 'sV', E.c['sV'])
                    E.dG('qCC', d_sc, acc)
                    E.wait('gpsimd', 'qCC', E.c['qCC'])
                    ld = E.dG('qCC', row_sb[:],
                              d_sc.rearrange("p one -> (one p)").unsqueeze(0))
                    E.wait('vector', 'qCC', ld)
                    E.vser(lambda e: e.reduce_max(one_sb[0:1, slot:slot + 1],
                                                  row_sb[:], AX.X))
                E.wait('gpsimd', 'sV', E.c['sV'])
                E.dG('qCC', d_ar_in[ar_idx][0:1, 0:2], one_sb[0:1, 0:2])
                E.wait('gpsimd', 'qCC', E.c['qCC'])
                E.g(lambda e: e.collective_compute(
                    "AllReduce", ALU.max, replica_groups=[list(range(NCORES))],
                    ins=[d_ar_in[ar_idx].opt()], outs=[d_ar_out[ar_idx].opt()]),
                    inc='cc', k=1)
                E.wait('gpsimd', 'cc', E.c['cc'])
                for slot, (s_ap, r_ap, qmax), gb in ((0, outs1, GBUF),
                                                     (1, outs2, GBUF2)):
                    E.wait('gpsimd', 'sV', E.c['sV'])
                    ldb = E.dG('qCC', gb, d_ar_out[ar_idx][0:1, slot:slot + 1]
                               .broadcast_to([128, 1]))
                    E.wait('vector', 'qCC', ldb)
                    E.vser(lambda e: e.tensor_scalar(s_ap, gb, 1.0 / qmax, 1e-12,
                                                     ALU.mult, ALU.max))
                    E.vser(lambda e: e.reciprocal(r_ap, s_ap))
                sn['scale', ar_idx] = E.c['sV']

            allreduce_scalar(m0acc[:], 0, (S0, R0, 32767.0))
            E.barrier()

            E.dS('qW', wu_sb[:].rearrange("p (c o) -> p c o", c=HC),
                 d_wut.transpose([1, 0, 2]))
            lw = E.dS('qW', wg_sb[:].rearrange("p (c o) -> p c o", c=HC),
                      d_wgt.transpose([1, 0, 2]))
            sn['wug'] = lw

            def b_prep(g):
                for tt4 in range(4):
                    ti = g * 4 + tt4
                    par = ti % 2
                    E.wait('sync', 'sV', sn.get(('xq', ti - 2), 0))
                    ld = E.dS(f'qLN{par}', lnbuf[:, par * H:(par + 1) * H],
                              d_ln2[ti * 128:(ti + 1) * 128, :])
                    E.wait('vector', f'qLN{par}', ld)
                    E.wait('vector', 'sT', sn.get(('xqtr', ti - 2), 0))
                    E.vser(lambda e: e.tensor_scalar(tmpq[:],
                                                     lnbuf[:, par * H:(par + 1) * H],
                                                     R0, MAGIC, ALU.mult, ALU.add))
                    E.vser(lambda e: e.tensor_scalar(xqtm[:, par * H:(par + 1) * H],
                                                     tmpq[:], MAGIC, S0,
                                                     ALU.subtract, ALU.mult))
                    sn['xq', ti] = E.c['sV']
                    pp = ptrot[0] % 2
                    E.wait('tensor', 'sV', sn['xq', ti])
                    if ptrot[0] - 2 in ptfree:
                        E.wait('tensor', 'sV', ptfree[ptrot[0] - 2])
                    for kc in range(HC):
                        E.t(lambda e: e.transpose(
                            psPT[pp][:, kc * 128:(kc + 1) * 128],
                            xqtm[:, par * H + kc * 128: par * H + (kc + 1) * 128],
                            eye_sb[:]), inc=(kc == HC - 1))
                    tsn = E.c['sT']
                    E.wait('vector', 'sT', tsn)
                    E.wait('vector', 'qXQS', sn.get(('xqsp', g - 1), 0))
                    dst3 = xqfm[:, 0:HC * 512].rearrange(
                        "p (kc f) -> p kc f", f=512)[
                        :, :, tt4 * 128:(tt4 + 1) * 128]
                    E.v(lambda e: e.tensor_copy(
                        dst3, psPT[pp][:].rearrange("p (c f) -> p c f", f=128)),
                        inc=True)
                    ptfree[ptrot[0]] = E.c['sV']
                    ptrot[0] += 1
                    sn['xqtr', ti] = E.c['sT']
                sn['xqfm', g] = E.c['sV']
                E.wait('gpsimd', 'sV', sn['xqfm', g])
                sp = E.dG('qXQS', d_xq[g], xqfm[:, 0:HC * 512])
                sn['xqsp', g] = sp

            def b_mms_one(j, w_sb, dsp, acc, rhs_base, ld_sem, ld_val,
                          hook=None):
                for ic in range(IC):
                    if hook is not None:
                        hook()
                    rot = ps6rot[0]
                    pb = ps6(rot)
                    ps6_wait(rot)
                    if ic == 0:
                        E.wait('tensor', ld_sem, ld_val)
                        if j == 0:
                            E.wait('tensor', 'qW', sn['wug'])
                    for kc in range(HC):
                        E.t(lambda e: e.matmul(
                            pb,
                            w_sb[:, kc * II + ic * 128: kc * II + ic * 128 + 128],
                            xqfm[:, rhs_base + kc * 512: rhs_base + (kc + 1) * 512],
                            start=(kc == 0), stop=(kc == HC - 1)),
                            inc=(kc == HC - 1))
                    tsn = E.c['sT']
                    srot = rot % 6
                    E.wait('scalar', 'sT', tsn)
                    E.wait('scalar', f'qSG{srot}', sn.get(('bspill', rot - 6), 0))
                    E.wait('scalar', 'sV', sn.get(('bred', rot - 6), 0))
                    E.s(lambda e: e.activation(stg[:, srot * 512:(srot + 1) * 512],
                                               pb, AF.Identity,
                                               bias=0.0, scale=1.0), inc=True)
                    ps6free[rot] = ('sS', E.c['sS'])
                    ev = E.c['sS']
                    E.wait('vector', 'sS', ev)
                    E.vser(lambda e: e.reduce_max(
                        stmx[:, srot:srot + 1],
                        stg[:, srot * 512:(srot + 1) * 512],
                        AX.X, apply_absolute_value=True))
                    sn['bred', rot] = E.c['sV']
                    if srot == 5 or ic == IC - 1:
                        lo = (srot // 6) * 6
                        E.vser(lambda e: e.reduce_max(redt[:, 0:1],
                                                      stmx[:, lo:srot + 1], AX.X))
                        E.vser(lambda e: e.tensor_tensor(acc[:], acc[:],
                                                         redt[:, 0:1], ALU.max))
                        sn['bred', rot] = E.c['sV']
                    E.wait('gpsimd', 'sS', ev)
                    sp = E.dG(f'qSG{srot}', dsp[ic * 128:(ic + 1) * 128,
                                                j * 512:(j + 1) * 512],
                              stg[:, srot * 512:(srot + 1) * 512])
                    sn['bspill', rot] = sp
                    ps6rot[0] += 1

            for g in range(NGRP + 1):
                if g >= 1:
                    b_mms_one(g - 1, wg_sb, d_g, gmaxa, 0, 'sV',
                              sn.get(('xqfm', g - 1), 0))
                    sn['bmmT', g - 1] = E.c['sT']
                if g < NGRP:
                    b_prep(g)
            sn['gspills'] = {i: E.c[f'qSG{i}'] for i in range(4)}

            allreduce_scalar(gmaxa[:], 1, (SG, RG, 32767.0))
            E.vser(lambda e: e.tensor_scalar(scl[:, 12:13], SG, 1.0 / 65536.0,
                                             None, ALU.mult))
            sn['sg216'] = E.c['sV']
            SG216 = scl[:, 12:13]

            def b2_load(g):
                par = (g + 1) % 2
                if g == 0:
                    E.wait('sync', 'qXQS', sn['xqsp', 0])
                elif g == 1:
                    E.wait('sync', 'sT', sn.get(('bmmT', NGRP - 1), 0))
                    E.wait('sync', 'qXQS', sn['xqsp', NGRP - 1])
                else:
                    E.wait('sync', 'sT', sn['b2mmT', g - 2])
                ld = E.dS(f'qXQ{par}', xqfm[:, par * HC * 512:(par + 1) * HC * 512],
                          d_xq[g])
                sn['b2ld', g] = ld

            def c_tile_gen(idx):
                ic, sl = idx // NSL, idx % NSL
                par = idx % 2
                if idx == 0:
                    for i4 in range(4):
                        E.wait('sync', f'qSG{i4}', sn['gspills'][i4])
                    E.wait('vector', 'sV', sn['sg216'])
                    E.wait('scalar', 'sV', sn['sg216'])
                gb = cbuf[:, par * 2048:(par + 1) * 2048]
                E.wait('sync', 'sV', sn.get(('cgq', idx - 2), 0))
                ld = E.dS(f'qCB{par}', gb, d_g[ic * 128:(ic + 1) * 128,
                                               sl * 2048:(sl + 1) * 2048])
                E.wait('scalar', f'qCB{par}', ld)
                E.wait('scalar', 'sV', sn.get(('cgq', idx - 2), 0))
                E.sser(lambda e: e.activation(sigb[:, par * 2048:(par + 1) * 2048],
                                              gb, AF.Identity, bias=MAGIC, scale=RG))
                sn['cc1', idx] = E.c['sS']
                yield
                E.wait('vector', 'sS', sn['cc1', idx])
                E.vser(lambda e: e.tensor_scalar(
                    c1[:], sigb[:, par * 2048:(par + 1) * 2048], MAGIC, SG216,
                    ALU.subtract, ALU.mult))
                sn['cgq', idx] = E.c['sV']
                yield
                E.wait('scalar', 'sV', sn['cgq', idx])
                E.wait('scalar', 'sV', sn.get(('ctV', idx - 1), 0))
                E.sser(lambda e: e.activation(c2[:], c1[:], AF.Sigmoid,
                                              scale=65536.0))
                yield
                E.sser(lambda e: e.activation(ngb[:], c2[:], AF.Identity,
                                              bias=MAGIC, scale=65536.0))
                sn['csig', idx] = E.c['sS']
                yield
                E.wait('vector', 'sS', sn['csig', idx])
                E.vser(lambda e: e.tensor_scalar(ngb[:], ngb[:], MAGIC, 65535.0,
                                                 ALU.subtract, ALU.min))
                sn['ckq', idx] = E.c['sV']
                yield
                tbv = tbb[:, par * 2048:(par + 1) * 2048]
                E.wait('vector', f'qTS{par}', sn.get(('ctsp', idx - 2), 0))
                E.vser(lambda e: e.tensor_tensor(tbv, c1[:], ngb[:], ALU.mult))
                yield
                E.vser(lambda e: e.reduce_max(redt[:, 0:1], tbv, AX.X,
                                              apply_absolute_value=True))
                E.vser(lambda e: e.tensor_tensor(tmaxa[:], tmaxa[:], redt[:, 0:1],
                                                 ALU.max))
                stv = E.c['sV']
                sn['ctV', idx] = stv
                E.wait('gpsimd', 'sV', stv)
                sp = E.dG(f'qTS{par}',
                          d_t[ic * 128:(ic + 1) * 128, sl * 2048:(sl + 1) * 2048],
                          tbv)
                sn['ctsp', idx] = sp
                yield

            def all_c_gen():
                for _ in range(12):
                    yield
                for idx in range(IC * NSL):
                    yield from c_tile_gen(idx)
            cgen = [None]

            def emit_c_tiles(n):
                if cgen[0] is None:
                    cgen[0] = all_c_gen()
                for _ in range(n):
                    next(cgen[0], None)

            for g in range(NGRP + 1):
                if g >= 1:
                    j = g - 1
                    b_mms_one(j, wu_sb, d_u, umaxa, ((j + 1) % 2) * HC * 512,
                              f'qXQ{(j + 1) % 2}', sn['b2ld', j],
                              hook=lambda: emit_c_tiles(1))
                    sn['b2mmT', j] = E.c['sT']
                if g < NGRP:
                    b2_load(g)
            emit_c_tiles(IC * NSL * 8)

            allreduce_pair(umaxa[:], tmaxa[:], 2, (SU, RU, 32767.0),
                           (ST_, RT_, 32767.0))
            E.vser(lambda e: e.tensor_tensor(STSU, ST_, SU, ALU.mult))
            E.barrier()

            NW = 4096 if NSL % 2 == 0 else 2048
            nh = max(1, (NSL * 2048) // NW)
            for idx in range(IC * nh):
                ic, s2 = idx // nh, idx % nh
                par = idx % 2
                c0 = s2 * NW
                tbv = tb2[:, par * NW:(par + 1) * NW]
                ubv = ub2[:, par * NW:(par + 1) * NW]
                cb = cb2[:, par * NW:(par + 1) * NW]
                ub1 = sg2[:, par * NW:(par + 1) * NW]
                ntv = nt2[:, par * NW:(par + 1) * NW]
                nuv = nu2[:, par * NW:(par + 1) * NW]
                hw2 = NW // 2
                E.wait('sync', 'sS', sn.get(('dtS', idx - 2), 0))
                ldta = E.dS(f'qTB{par}', tbv[:, 0:hw2],
                            d_t[ic * 128:(ic + 1) * 128, c0:c0 + hw2])
                ldtb = E.dS(f'qTC{par}', tbv[:, hw2:NW],
                            d_t[ic * 128:(ic + 1) * 128, c0 + hw2:c0 + NW])
                E.wait('sync', 'sS', sn.get(('duS', idx - 2), 0))
                ldua = E.dS(f'qUB{par}', ubv[:, 0:hw2],
                            d_u[ic * 128:(ic + 1) * 128, c0:c0 + hw2])
                ldub = E.dS(f'qUC{par}', ubv[:, hw2:NW],
                            d_u[ic * 128:(ic + 1) * 128, c0 + hw2:c0 + NW])
                E.wait('scalar', f'qTB{par}', ldta)
                E.wait('scalar', f'qTC{par}', ldtb)
                E.wait('scalar', 'sV', sn.get(('dzv', idx - 2), 0))
                E.wait('scalar', f'qNT{par}', sn.get(('dzsp', idx - 2), 0))
                E.wait('scalar', f'qNC{par}', sn.get(('dzsp2', idx - 2), 0))
                E.sser(lambda e: e.activation(cb, tbv, AF.Identity, bias=MAGIC,
                                              scale=RT_))
                sn['dtS', idx] = E.c['sS']
                E.wait('scalar', f'qUB{par}', ldua)
                E.wait('scalar', f'qUC{par}', ldub)
                E.sser(lambda e: e.activation(ub1, ubv, AF.Identity, bias=MAGIC,
                                              scale=RU))
                E.sser(lambda e: e.activation(nuv, ub1, AF.Identity, bias=-MAGIC,
                                              scale=1.0))
                sn['duS', idx] = E.c['sS']
                E.wait('vector', 'sS', sn['dtS', idx])
                E.vser(lambda e: e.tensor_scalar(ntv, cb, MAGIC, STSU,
                                                 ALU.subtract, ALU.mult))
                E.wait('vector', 'sS', sn['duS', idx])
                E.vser(lambda e: e.tensor_tensor(cb, ntv, nuv, ALU.mult))
                E.vser(lambda e: e.reduce_max(redt[:, 1:2], cb, AX.X,
                                              apply_absolute_value=True))
                E.vser(lambda e: e.tensor_tensor(zmaxa[:], zmaxa[:], redt[:, 1:2],
                                                 ALU.max))
                sn['dzv', idx] = E.c['sV']
                E.wait('gpsimd', 'sV', sn['dzv', idx])
                spza = E.dG(f'qNT{par}',
                            d_z[ic * 128:(ic + 1) * 128, c0:c0 + hw2],
                            cb[:, 0:hw2])
                spzb = E.dG(f'qNC{par}',
                            d_z[ic * 128:(ic + 1) * 128, c0 + hw2:c0 + NW],
                            cb[:, hw2:NW])
                sn['dzsp', idx] = spza
                sn['dzsp2', idx] = spzb
                
            allreduce_scalar(zmaxa[:], 3, (SZ, RZ, 32767.0))
            E.barrier()

            lw = E.dS('qW', wd_sb[:].rearrange("p (c o) -> p c o", c=IC),
                      d_wdt.transpose([1, 0, 2]))
            sn['wd'] = lw

            def e_prep_group(sl, g4):
                third = g4 // 2
                if g4 % 2 == 0:
                    hidx = sl * 3 + third
                    hpar = hidx % 2
                    r0 = third * 8 * 128
                    E.wait('gpsimd', 'sS', sn.get(('ehalf', hidx - 2), 0))
                    ldz = E.dG(f'qEN{hpar}',
                               zb_e[:, hpar * 4096:(hpar + 1) * 4096]
                               .rearrange("p (c t) -> p c t", t=512),
                               d_z[r0:r0 + 8 * 128, sl * 512:(sl + 1) * 512]
                               .rearrange("(c p) t -> p c t", p=128))
                    sn['eld', hidx] = ldz
                hidx = sl * 3 + third
                hpar = hidx % 2
                gidx = sl * 6 + g4
                gpar = gidx % 2
                inoff = hpar * 4096 + (g4 % 2) * 2048
                eos = eo1[:, gpar * 2048:(gpar + 1) * 2048]
                E.wait('scalar', f'qEN{hpar}', sn['eld', hidx])
                E.wait('scalar', 'sV', sn.get(('eo', gidx - 2), 0))
                E.sser(lambda e: e.activation(eos, zb_e[:, inoff:inoff + 2048],
                                              AF.Identity, bias=MAGIC, scale=RZ))
                sez = E.c['sS']
                if g4 % 2 == 1:
                    sn['ehalf', hidx] = sez
                E.wait('vector', 'sS', sez)
                E.wait('vector', 'sT', sn.get(('emm', sl - 2), 0))
                E.vser(lambda e: e.tensor_scalar(
                    ofm[:, (sl % 2) * IC * 512 + g4 * 2048:
                        (sl % 2) * IC * 512 + (g4 + 1) * 2048],
                    eos, MAGIC, SZ, ALU.subtract, ALU.mult))
                sn['eo', gidx] = E.c['sV']
                if g4 == 5:
                    sn['eofm', sl] = E.c['sV']

            def e_mms_t(j, tt4):
                obase = (j % 2) * IC * 512
                if True:
                    ti = j * 4 + tt4
                    parx = ti % 2
                    E.wait('sync', f'qY{parx}', sn.get(('yst', ti - 2), 0))
                    ldx = E.dS(f'qXB{parx}', x1b[:, parx * H:(parx + 1) * H],
                               d_x1[ti * 128:(ti + 1) * 128, :])
                    for hb in range(2):
                        rot = ps6rot[0]
                        pb = ps6(rot)
                        ps6_wait(rot)
                        E.wait('tensor', 'sV', sn['eofm', j])
                        if j == 0 and tt4 == 0 and hb == 0:
                            E.wait('tensor', 'qW', sn['wd'])
                        for ic in range(IC):
                            E.t(lambda e: e.matmul(
                                pb,
                                ofm[:, obase + ic * 512 + tt4 * 128:
                                    obase + ic * 512 + tt4 * 128 + 128],
                                wd_sb[:, ic * H + hb * 512: ic * H + hb * 512 + 512],
                                start=(ic == 0), stop=(ic == IC - 1)),
                                inc=(ic == IC - 1))
                        tsn = E.c['sT']
                        E.wait('vector', 'sT', tsn)
                        E.wait('vector', f'qXB{parx}', ldx)
                        E.wait('vector', f'qY{parx}', sn.get(('yst', ti - 2), 0))
                        E.v(lambda e: e.scalar_tensor_tensor(
                            y_sb[:, parx * H + hb * 512: parx * H + hb * 512 + 512],
                            pb, 1.0,
                            x1b[:, parx * H + hb * 512: parx * H + hb * 512 + 512],
                            ALU.mult, ALU.add), inc=True)
                        ps6free[rot] = ('sV', E.c['sV'])
                        ps6rot[0] += 1
                    yv = E.c['sV']
                    E.wait('sync', 'sV', yv)
                    st = E.dS(f'qY{parx}', d_y[ti * 128:(ti + 1) * 128, :],
                              y_sb[:, parx * H:(parx + 1) * H])
                    sn['yst', ti] = st
                if tt4 == 3:
                    sn['emm', j] = E.c['sT']

            for sl in range(NGRP + 1):
                steps = []
                if sl < NGRP:
                    steps += [('p', sl, 0), ('p', sl, 1)]
                if sl >= 1:
                    steps += [('m', sl - 1, 0)]
                if sl < NGRP:
                    steps += [('p', sl, 2)]
                if sl >= 1:
                    steps += [('m', sl - 1, 1)]
                if sl < NGRP:
                    steps += [('p', sl, 3)]
                if sl >= 1:
                    steps += [('m', sl - 1, 2)]
                if sl < NGRP:
                    steps += [('p', sl, 4), ('p', sl, 5)]
                if sl >= 1:
                    steps += [('m', sl - 1, 3)]
                for kind, a, b2 in steps:
                    if kind == 'p':
                        e_prep_group(a, b2)
                    else:
                        e_mms_t(a, b2)

            E.barrier()

        with nc.Block() as block:
            block.sync(lambda eng: walk(Em('sync', eng, sems)))
            block.gpsimd(lambda eng: walk(Em('gpsimd', eng, sems)))
            block.tensor(lambda eng: walk(Em('tensor', eng, sems)))
            block.vector(lambda eng: walk(Em('vector', eng, sems)))
            block.scalar(lambda eng: walk(Em('scalar', eng, sems)))

    return nc


def kernel(L0_attn_x, L0_attn_Wq, L0_attn_bq, L0_attn_Wk, L0_attn_bk,
           L0_attn_Wv, L0_attn_bv, L0_ffn_w_gate, L0_ffn_w_up, L0_ffn_w_down):
    stage = os.environ.get("ATH_STAGE", "")
    f32 = np.float32
    x = np.asarray(L0_attn_x, dtype=f32)
    wqt = np.ascontiguousarray(np.asarray(L0_attn_Wq, dtype=f32).T)
    wkt = np.ascontiguousarray(np.asarray(L0_attn_Wk, dtype=f32).T)
    wvt = np.ascontiguousarray(np.asarray(L0_attn_Wv, dtype=f32).T)
    bq2 = np.ascontiguousarray(np.asarray(L0_attn_bq, dtype=f32).reshape(HC, 128).T)
    bk2 = np.ascontiguousarray(np.asarray(L0_attn_bk, dtype=f32).reshape(HC, 128).T)
    bv_row = np.ascontiguousarray(np.asarray(L0_attn_bv, dtype=f32).reshape(1, H))
    w_up = np.ascontiguousarray(np.asarray(L0_ffn_w_up, dtype=f32))
    w_gate = np.ascontiguousarray(np.asarray(L0_ffn_w_gate, dtype=f32))
    w_down = np.ascontiguousarray(np.asarray(L0_ffn_w_down, dtype=f32))
    eye = np.eye(128, dtype=f32)
    negeye = np.asarray(np.eye(128) * NEG_BIG, dtype=f32)

    nc = build_program(stage)
    in_maps = []
    for c in range(NCORES):
        xs = np.ascontiguousarray(x[c * BPC:(c + 1) * BPC].reshape(T, H))
        in_maps.append(dict(x=xs, wqt=wqt, wkt=wkt, wvt=wvt, bq2=bq2, bk2=bk2,
                            bv_row=bv_row, w_up=w_up, w_gate=w_gate,
                            w_down=w_down, eye=eye, negeye=negeye))
    res = run_bass_kernel_spmd(nc, in_maps, core_ids=list(range(NCORES)))
    y = np.concatenate([r["y"].reshape(BPC, L, H) for r in res.results], axis=0)
    return y


# version 22 (final)
# speedup vs baseline: 1.1916x; 1.0038x over previous
import os
from contextlib import ExitStack

import numpy as np

import concourse.bass as bass
import concourse.mybir as mybir
from concourse.bass_utils import run_bass_kernel_spmd

F32 = mybir.dt.float32
BF16 = mybir.dt.bfloat16
I16 = mybir.dt.int16
I32 = mybir.dt.int32
AF = mybir.ActivationFunctionType
ALU = mybir.AluOpType
AX = mybir.AxisListType

MAGIC = float(np.float32(1.5 * 2 ** 23))
RSQRT_SEED = 0x5F3759DF
NEG_BIG = float(np.float32(-1.0e38))
EPS = 1e-5

NCORES = 8
B, L, H, II = 32, 2048, 1024, 3072
BPC = int(os.environ.get("ATH_BPC", str(B // NCORES)))
T = BPC * L
HC = H // 128
IC = II // 128
LT = L // 128
NGRP = T // 512
NSL = T // 2048

ENG_SEM = {'tensor': 'sT', 'vector': 'sV', 'scalar': 'sS'}


class Em:

    def __init__(self, which, eng, sems):
        self.which = which
        self.eng = eng
        self.sems = sems
        self.c = {k: 0 for k in sems}

    def _run(self, name, fn, inc=None, k=1):
        inst = fn(self.eng) if self.which == name else None
        if inc is not None:
            self.c[inc] += k
            if inst is not None:
                inst.then_inc(self.sems[inc], k)
        return self.c[inc] if inc else 0

    def dS(self, sem, out, in_):
        return self._run('sync', lambda e: e.dma_start(out=out, in_=in_), sem, 16)

    def dG(self, sem, out, in_):
        return self._run('gpsimd', lambda e: e.dma_start(out=out, in_=in_), sem, 16)

    def t(self, fn, inc=False):
        return self._run('tensor', fn, 'sT' if inc else None)

    def v(self, fn, inc=False):
        return self._run('vector', fn, 'sV' if inc else None)

    def s(self, fn, inc=False):
        return self._run('scalar', fn, 'sS' if inc else None)

    def g(self, fn, inc=None, k=1):
        return self._run('gpsimd', fn, inc, k)

    def wait(self, name, ctr, val):
        if self.which == name and val > 0:
            self.eng.wait_ge(self.sems[ctr], val)

    def vser(self, fn):
        self.wait('vector', 'sV', self.c['sV'])
        return self.v(fn, inc=True)

    def sser(self, fn):
        self.wait('scalar', 'sS', self.c['sS'])
        return self.s(fn, inc=True)

    def barrier(self):
        own = ENG_SEM.get(self.which)
        for ctr in self.sems:
            if ctr != own:
                self.wait(self.which, ctr, self.c[ctr])


def build_program(stage=""):
    nc = bass.Bass("TRN2", target_bir_lowering=False)

    d_x = nc.dram_tensor("x", [T, H], F32, kind="ExternalInput").ap()
    d_wqt = nc.dram_tensor("wqt", [H, H], F32, kind="ExternalInput").ap()
    d_wkt = nc.dram_tensor("wkt", [H, H], F32, kind="ExternalInput").ap()
    d_wvt = nc.dram_tensor("wvt", [H, H], F32, kind="ExternalInput").ap()
    d_bq = nc.dram_tensor("bq2", [128, HC], F32, kind="ExternalInput").ap()
    d_bk = nc.dram_tensor("bk2", [128, HC], F32, kind="ExternalInput").ap()
    d_bv = nc.dram_tensor("bv_row", [1, H], F32, kind="ExternalInput").ap()
    d_wu = nc.dram_tensor("w_up", [II, H], F32, kind="ExternalInput").ap()
    d_wg = nc.dram_tensor("w_gate", [II, H], F32, kind="ExternalInput").ap()
    d_wd = nc.dram_tensor("w_down", [H, II], F32, kind="ExternalInput").ap()
    d_eye = nc.dram_tensor("eye", [128, 128], F32, kind="ExternalInput").ap()
    d_negeye = nc.dram_tensor("negeye", [128, 128], F32, kind="ExternalInput").ap()
    d_y = nc.dram_tensor("y", [T, H], F32, kind="ExternalOutput").ap()

    d_x1 = nc.dram_tensor("x1_sp", [T, H], F32).ap()
    d_ln2 = nc.dram_tensor("ln2_sp", [T, H], F32).ap()
    d_u = nc.dram_tensor("u_sp", [II, T], F32).ap()
    d_g = nc.dram_tensor("g_sp", [II, T], F32).ap()
    d_t = nc.dram_tensor("t_sp", [II, T], F32).ap()
    d_z = nc.dram_tensor("z_sp", [II, T], F32).ap()
    d_wut = nc.dram_tensor("wut_dq", [HC, 128, II], BF16).ap()
    d_wgt = nc.dram_tensor("wgt_dq", [HC, 128, II], BF16).ap()
    d_wdt = nc.dram_tensor("wdt_dq", [IC, 128, H], BF16).ap()
    d_xq = nc.dram_tensor("xq_sp", [NGRP, 128, HC * 512], BF16).ap()
    d_sc = nc.dram_tensor("sc_rt", [128, 1], F32).ap()
    d_ar_in = [nc.dram_tensor(f"ar{i}_in", [1, 2], F32).ap() for i in range(4)]
    d_ar_out = [nc.dram_tensor(f"ar{i}_out", [1, 2], F32).ap() for i in range(4)]

    with ExitStack() as top:
        E_ = top.enter_context

        sems = {}
        sem_names = ['sT', 'sV', 'sS', 'sG', 'cc', 'qPre', 'qWQ', 'qW', 'qCC',
                     'qWN0', 'qWN1', 'qST0', 'qST1', 'qX0', 'qX1', 'qX2', 'qX3',
                     'qSP0', 'qSP1', 'qLN0', 'qLN1',
                     'qSG0', 'qSG1', 'qSG2', 'qSG3', 'qSG4', 'qSG5',
                     'qCB0', 'qCB1', 'qTS0', 'qTS1',
                     'qTB0', 'qTB1', 'qUB0', 'qUB1',
                     'qTC0', 'qTC1', 'qUC0', 'qUC1', 'qNC0', 'qNC1',
                     'qNT0', 'qNT1', 'qNU0', 'qNU1',
                     'qEN0', 'qEN1', 'qXB0', 'qXB1', 'qY0', 'qY1', 'qXQ0', 'qXQ1', 'qXQS']
        for name in sem_names:
            sems[name] = E_(nc.semaphore(name))

        psS = E_(nc.psum_tensor("psS", [128, 2048], F32))
        psA = E_(nc.psum_tensor("psA", [128, 1024], F32))
        psPT = [E_(nc.psum_tensor(f"psPT{i}", [128, 1024], BF16)) for i in range(2)]

        eye_sb = E_(nc.sbuf_tensor("eye_sb", [128, 128], BF16))
        negeye_sb = E_(nc.sbuf_tensor("negeye_sb", [128, 128], F32))
        bq_sb = E_(nc.sbuf_tensor("bq_sb", [128, HC], F32))
        bk_sb = E_(nc.sbuf_tensor("bk_sb", [128, HC], F32))
        bv_sb = E_(nc.sbuf_tensor("bv_sb", [128, H], BF16))
        cseed = E_(nc.sbuf_tensor("cseed", [128, 4], I32))
        cmagic = E_(nc.sbuf_tensor("cmagic", [128, 4], F32))
        nc.const_aps.register(F32, MAGIC, cmagic[:, 0:1])
        nc.const_aps.register(F32, 0.0, cmagic[:, 1:2])
        nc.const_aps.register(F32, EPS, cmagic[:, 2:3])
        nc.const_aps.register(F32, -MAGIC, cmagic[:, 3:4])
        scl = E_(nc.sbuf_tensor("scl", [128, 16], F32))
        stw = E_(nc.sbuf_tensor("stw", [128, 64], F32))
        row_sb = E_(nc.sbuf_tensor("row_sb", [1, 128], F32))
        one_sb = E_(nc.sbuf_tensor("one_sb", [1, 4], F32))
        m0acc = E_(nc.sbuf_tensor("m0acc", [128, 1], F32))
        umaxa = E_(nc.sbuf_tensor("umaxa", [128, 1], F32))
        gmaxa = E_(nc.sbuf_tensor("gmaxa", [128, 1], F32))
        tmaxa = E_(nc.sbuf_tensor("tmaxa", [128, 1], F32))
        zmaxa = E_(nc.sbuf_tensor("zmaxa", [128, 1], F32))
        redt = E_(nc.sbuf_tensor("redt", [128, 2], F32))

        att = ExitStack()
        A_ = att.enter_context
        wq_sb = A_(nc.sbuf_tensor("wq_sb", [128, HC * H], BF16))
        wk_sb = A_(nc.sbuf_tensor("wk_sb", [128, HC * H], BF16))
        wv_sb = A_(nc.sbuf_tensor("wv_sb", [128, HC * H], BF16))
        q_sb = A_(nc.sbuf_tensor("q_sb", [128, HC * L], BF16))
        k_sb = A_(nc.sbuf_tensor("k_sb", [128, HC * L], BF16))
        v_sb = A_(nc.sbuf_tensor("v_sb", [128, LT * H], BF16))
        lnfm = A_(nc.sbuf_tensor("lnfm", [128, HC * 512], BF16))
        P_sb = A_(nc.sbuf_tensor("P_sb", [128, 2 * 2048], BF16))
        pt_sb = A_(nc.sbuf_tensor("pt_sb", [128, 2048], BF16))
        x1_sb = A_(nc.sbuf_tensor("x1_sb", [128, 2 * H], F32))
        ln2_sb = A_(nc.sbuf_tensor("ln2_sb", [128, 2 * H], F32))
        xbuf = A_(nc.sbuf_tensor("xbuf", [128, 4 * H], F32))
        ln1tm = A_(nc.sbuf_tensor("ln1tm", [128, 2 * H], BF16))
        sqdead = A_(nc.sbuf_tensor("sqdead", [128, H], BF16))
        rmaxb = A_(nc.sbuf_tensor("rmaxb", [128, 2], F32))
        negb = A_(nc.sbuf_tensor("negb", [128, 2], F32))
        rsacc = A_(nc.sbuf_tensor("rsacc", [128, 2], F32))
        recipb = A_(nc.sbuf_tensor("recipb", [128, 2], F32))
        att.close()

        deq = ExitStack()
        D_ = deq.enter_context
        wnat = D_(nc.sbuf_tensor("wnat", [128, 2 * II], F32))
        dq1 = D_(nc.sbuf_tensor("dq1", [128, II], F32))
        dqr = D_(nc.sbuf_tensor("dqr", [128, II], F32))
        dqo = D_(nc.sbuf_tensor("dqo", [128, II], BF16))
        sblk = D_(nc.sbuf_tensor("sblk", [128, II // 32], F32))
        rblk = D_(nc.sbuf_tensor("rblk", [128, II // 32], F32))
        wstg = D_(nc.sbuf_tensor("wstg", [128, IC * H], BF16))
        deq.close()

        ffn = ExitStack()
        F_ = ffn.enter_context
        wu_sb = F_(nc.sbuf_tensor("wu_sb", [128, HC * II], BF16, side='right'))
        wg_sb = F_(nc.sbuf_tensor("wg_sb", [128, HC * II], BF16, side='right'))
        xqfm = F_(nc.sbuf_tensor("xqfm", [128, 2 * HC * 512], BF16, side='right'))
        stg = F_(nc.sbuf_tensor("stg", [128, 6 * 512], F32, side='right'))
        stmx = F_(nc.sbuf_tensor("stmx", [128, 6], F32, side='right'))
        b1t = ExitStack()
        B1_ = b1t.enter_context
        lnbuf = B1_(nc.sbuf_tensor("lnbuf", [128, 2 * H], F32))
        tmpq = B1_(nc.sbuf_tensor("tmpq", [128, H], F32))
        xqtm = B1_(nc.sbuf_tensor("xqtm", [128, 2 * H], BF16))
        b1t.close()
        cd = ExitStack()
        C_ = cd.enter_context
        cbuf = C_(nc.sbuf_tensor("cbuf", [128, 2 * 2048], F32))
        c1 = C_(nc.sbuf_tensor("c1", [128, 2048], F32))
        c2 = C_(nc.sbuf_tensor("c2", [128, 2048], F32))
        ngb = C_(nc.sbuf_tensor("ngb", [128, 2048], F32))
        sigb = C_(nc.sbuf_tensor("sigb", [128, 2 * 2048], F32))
        tbb = C_(nc.sbuf_tensor("tbb", [128, 2 * 2048], F32))
        ffn.close()
        cd.close()
        dsc = ExitStack()
        D_2 = dsc.enter_context
        tb2 = D_2(nc.sbuf_tensor("tb2", [128, 2 * 4096], F32))
        ub2 = D_2(nc.sbuf_tensor("ub2", [128, 2 * 4096], F32))
        cb2 = D_2(nc.sbuf_tensor("cb2", [128, 2 * 4096], F32))
        sg2 = D_2(nc.sbuf_tensor("sg2", [128, 2 * 4096], F32))
        nt2 = D_2(nc.sbuf_tensor("nt2", [128, 2 * 4096], F32))
        nu2 = D_2(nc.sbuf_tensor("nu2", [128, 2 * 4096], F32))
        dsc.close()

        eph = ExitStack()
        P2_ = eph.enter_context
        wd_sb = P2_(nc.sbuf_tensor("wd_sb", [128, IC * H], BF16))
        ofm = P2_(nc.sbuf_tensor("ofm", [128, 2 * IC * 512], BF16))
        zb_e = P2_(nc.sbuf_tensor("zb_e", [128, 2 * 4096], F32))
        eo1 = P2_(nc.sbuf_tensor("eo1", [128, 2 * 2048], F32))
        x1b = P2_(nc.sbuf_tensor("x1b", [128, 2 * H], F32))
        y_sb = P2_(nc.sbuf_tensor("y_sb", [128, 2 * H], F32))
        eph.close()

        S0, R0 = scl[:, 0:1], scl[:, 1:2]
        SU, RU = scl[:, 2:3], scl[:, 3:4]
        SG, RG = scl[:, 4:5], scl[:, 5:6]
        ST_, RT_ = scl[:, 6:7], scl[:, 7:8]
        SZ, RZ = scl[:, 8:9], scl[:, 9:10]
        GBUF, GBUF2 = scl[:, 10:11], scl[:, 11:12]
        STSU = scl[:, 13:14]

        def walk(E: Em):
            sn = {}

            def ydump(off, c):
                return d_y.rearrange("a h -> (a h)")[off:off + 128 * c] \
                    .rearrange("(p c) -> p c", p=128)

            E.dS('qW', bq_sb[:], d_bq)
            E.dS('qW', bk_sb[:], d_bk)
            E.dS('qW', negeye_sb[:], d_negeye)
            E.dG('qPre', eye_sb[:], d_eye)
            E.dG('qPre', bv_sb[:], d_bv.broadcast_to([128, H]))
            E.v(lambda e: e.memset(cseed[:], RSQRT_SEED))
            E.v(lambda e: e.memset(cmagic[:, 0:1], MAGIC))
            E.v(lambda e: e.memset(cmagic[:, 1:2], 0.0))
            E.v(lambda e: e.memset(cmagic[:, 2:3], EPS))
            E.v(lambda e: e.memset(cmagic[:, 3:4], -MAGIC))
            E.v(lambda e: e.memset(m0acc[:], 0.0))
            E.v(lambda e: e.memset(umaxa[:], 0.0))
            E.v(lambda e: e.memset(gmaxa[:], 0.0))
            E.v(lambda e: e.memset(tmaxa[:], 0.0))
            E.v(lambda e: e.memset(zmaxa[:], 0.0), inc=True)
            E.barrier()

            def dequant_weight(dw_nat, dw_out, O, K):
                OC, KC, KB = O // 128, K // 128, K // 32
                stq = sn.get('wstg_stored', 0)
                for oc in range(OC):
                    par = oc % 2
                    E.wait('sync', 'sV', sn.get(('dqfree', par), 0))
                    ld = E.dS(f'qWN{par}', wnat[:, par * II:par * II + K],
                              dw_nat[oc * 128:(oc + 1) * 128, :])
                    wn = wnat[:, par * II:par * II + K]
                    wn3 = wn.rearrange("p (nb b) -> p nb b", b=32)
                    E.wait('vector', f'qWN{par}', ld)
                    E.vser(lambda e: e.reduce_max(sblk[:, 0:KB], wn3, AX.X,
                                                  apply_absolute_value=True))
                    E.vser(lambda e: e.tensor_scalar(sblk[:, 0:KB], sblk[:, 0:KB],
                                                     1.0 / 7.0, 1e-12, ALU.mult,
                                                     ALU.max))
                    E.vser(lambda e: e.reciprocal(rblk[:, 0:KB], sblk[:, 0:KB]))
                    rb3 = rblk[:, 0:KB].rearrange("p nb -> p nb ()") \
                        .broadcast_to([128, KB, 32])
                    sb3 = sblk[:, 0:KB].rearrange("p nb -> p nb ()") \
                        .broadcast_to([128, KB, 32])
                    E.vser(lambda e: e.tensor_tensor(
                        dq1[:, 0:K].rearrange("p (nb b) -> p nb b", b=32), wn3, rb3,
                        ALU.mult))
                    sn['dqfree', par] = E.c['sV']
                    E.vser(lambda e: e.tensor_scalar(dq1[:, 0:K], dq1[:, 0:K],
                                                     MAGIC, MAGIC, ALU.add,
                                                     ALU.subtract))
                    E.wait('vector', 'sT', sn.get('dqtrl', 0))
                    E.vser(lambda e: e.tensor_tensor(
                        dqo[:, 0:K].rearrange("p (nb b) -> p nb b", b=32),
                        dq1[:, 0:K].rearrange("p (nb b) -> p nb b", b=32), sb3,
                        ALU.mult))
                    dq_ready = E.c['sV']
                    wst3 = wstg[:, 0:KC * O].rearrange("p (kc o) -> p kc o", o=O)
                    for rnd in range(KC // 8):
                        pp = rnd % 2
                        E.wait('tensor', 'sS', sn.get(('ptev', pp), 0))
                        E.wait('tensor', 'sV', dq_ready)
                        for s in range(8):
                            kc = rnd * 8 + s
                            E.t(lambda e: e.transpose(
                                psPT[pp][:, s * 128:(s + 1) * 128],
                                dqo[:, kc * 128:(kc + 1) * 128],
                                eye_sb[:]), inc=(s == 7))
                        tsn = E.c['sT']
                        E.wait('scalar', 'sT', tsn)
                        if oc == 0 and rnd == 0:
                            E.wait('scalar', 'qST0', stq)
                        E.sser(lambda e: e.activation(
                            wst3[:, rnd * 8:(rnd + 1) * 8, oc * 128:(oc + 1) * 128],
                            psPT[pp][:].rearrange("p (c f) -> p c f", f=128),
                            AF.Identity, bias=0.0, scale=1.0))
                        sn['ptev', pp] = E.c['sS']
                    sn['dqtrl'] = E.c['sT']
                ev_all = E.c['sS']
                E.wait('gpsimd', 'sS', ev_all)
                for kc in range(KC):
                    st = E.dG('qST0', dw_out[kc, :, :], wstg[:, kc * O:(kc + 1) * O])
                sn['wstg_stored'] = E.c['qST0']

            dequant_weight(d_wu, d_wut, II, H)
            dequant_weight(d_wg, d_wgt, II, H)
            dequant_weight(d_wd, d_wdt, H, II)
            E.barrier()

            w2rot = [0]
            w2free = {}
            ptrot = [0]
            ptfree = {}

            def psa_wait(rot):
                ent = w2free.get(rot - 2)
                if ent is not None:
                    E.wait('tensor', ent[0], ent[1])

            ps6rot = [0]
            ps6free = {}

            def ps6(rot):
                k = rot % 6
                if k < 2:
                    return psA[:, k * 512:(k + 1) * 512]
                return psS[:, (k - 2) * 512:(k - 1) * 512]

            def ps6_wait(rot):
                ent = ps6free.get(rot - 6)
                if ent is not None:
                    E.wait('tensor', ent[0], ent[1])

            def qkv_group(b, g):
                for tt4 in range(4):
                    i = g * 4 + tt4
                    gi = b * 16 + i
                    slot = tt4
                    if g >= 1:
                        E.wait('sync', 'sS', sn.get(('ln1', b, i - 4), 0))
                    elif b > 0:
                        E.wait('sync', 'sV', sn.get(('x1ev', b - 1, 12 + tt4), 0))
                    ld = E.dS(f'qX{slot}', xbuf[:, slot * H:(slot + 1) * H],
                              d_x[gi * 128:(gi + 1) * 128, :])
                    sn['ldx', b, i] = ld
                    E.wait('vector', f'qX{slot}', ld)
                    E.v(lambda e: e.reduce_sum(stw[:, 32 + tt4:33 + tt4],
                                               xbuf[:, slot * H:(slot + 1) * H],
                                               AX.X), inc=True)
                    E.wait('scalar', f'qX{slot}', ld)
                    if tt4 == 0:
                        E.wait('scalar', 'sV', sn.get('ch1prev', 0))
                    E.sser(lambda e: e.activation(sqdead[:],
                                                  xbuf[:, slot * H:(slot + 1) * H],
                                                  AF.Square,
                                                  accum_out=stw[:, 36 + tt4:37 + tt4]))
                sn['sq4', b, g] = E.c['sS']
                sn['rs4', b, g] = E.c['sV']
                m2 = stw[:, 40:44]
                esq = stw[:, 44:48]
                var = stw[:, 48:52]
                rstd = stw[:, 52:56]
                negmrs = stw[:, 56:60]
                lnv = stw[:, 60:64]
                E.wait('scalar', 'sV', sn['rs4', b, g])
                E.sser(lambda e: e.activation(m2, stw[:, 32:36], AF.Square,
                                              scale=1.0 / H))
                E.sser(lambda e: e.activation(esq, stw[:, 36:40], AF.Identity,
                                              bias=0.0, scale=1.0 / H))
                sms = E.c['sS']
                E.wait('vector', 'sS', sms)
                E.vser(lambda e: e.tensor_tensor(var, esq, m2, ALU.subtract))
                svv = E.c['sV']
                E.wait('scalar', 'sV', svv)
                E.sser(lambda e: e.activation(lnv, var, AF.Ln, bias=EPS))
                E.sser(lambda e: e.activation(rstd, lnv, AF.Exp, scale=-0.5))
                srs = E.c['sS']
                E.wait('vector', 'sS', srs)
                E.vser(lambda e: e.scalar_tensor_tensor(
                    negmrs, stw[:, 32:36], -1.0 / H, rstd, ALU.mult, ALU.mult))
                sn['ch1', b, g] = E.c['sV']
                sn['ch1prev'] = E.c['sV']
                for tt4 in range(4):
                    i = g * 4 + tt4
                    par = i % 2
                    slot = tt4
                    E.wait('scalar', 'sV', sn['ch1', b, g])
                    E.wait('scalar', 'sT', sn.get(('ln1tr', b, i - 2), 0))
                    E.s(lambda e: e.activation(ln1tm[:, par * H:(par + 1) * H],
                                               xbuf[:, slot * H:(slot + 1) * H],
                                               AF.Identity,
                                               bias=negmrs[:, tt4:tt4 + 1],
                                               scale=rstd[:, tt4:tt4 + 1]), inc=True)
                    sn['ln1', b, i] = E.c['sS']
                    pp = ptrot[0] % 2
                    E.wait('tensor', 'sS', sn['ln1', b, i])
                    if ptrot[0] - 2 in ptfree:
                        E.wait('tensor', 'sV', ptfree[ptrot[0] - 2])
                    for kc in range(HC):
                        E.t(lambda e: e.transpose(
                            psPT[pp][:, kc * 128:(kc + 1) * 128],
                            ln1tm[:, par * H + kc * 128: par * H + (kc + 1) * 128],
                            eye_sb[:]), inc=(kc == HC - 1))
                    tsn = E.c['sT']
                    E.wait('vector', 'sT', tsn)
                    dst3 = lnfm.rearrange("p (kc f) -> p kc f", f=512)[
                        :, :, tt4 * 128:(tt4 + 1) * 128]
                    E.v(lambda e: e.tensor_copy(
                        dst3, psPT[pp][:].rearrange("p (c f) -> p c f", f=128)),
                        inc=True)
                    ptfree[ptrot[0]] = E.c['sV']
                    ptrot[0] += 1
                    sn['ln1tr', b, i] = E.c['sT']
                sn['lnfm', b, g] = E.c['sV']

            def qkv_mms(b, j):
                gbase = 0
                if j == 0:
                    E.wait('tensor', 'qWQ', sn['wqkv', b])
                for (w_sb, bias_sb, dst_sb) in ((wq_sb, bq_sb, q_sb),
                                                (wk_sb, bk_sb, k_sb)):
                    for oc in range(HC):
                        rot = w2rot[0]
                        half = (rot % 2) * 512
                        psa_wait(rot)
                        E.wait('tensor', 'sV', sn['lnfm', b, j])
                        for kc in range(HC):
                            E.t(lambda e: e.matmul(
                                psA[:, half:half + 512],
                                w_sb[:, kc * H + oc * 128: kc * H + oc * 128 + 128],
                                lnfm[:, gbase + kc * 512: gbase + (kc + 1) * 512],
                                start=(kc == 0), stop=(kc == HC - 1)),
                                inc=(kc == HC - 1))
                        tsn = E.c['sT']
                        E.wait('scalar', 'sT', tsn)
                        col = oc * L + j * 512
                        E.s(lambda e: e.activation(dst_sb[:, col:col + 512],
                                                   psA[:, half:half + 512], AF.Identity,
                                                   bias=bias_sb[:, oc:oc + 1],
                                                   scale=1.0), inc=True)
                        w2free[rot] = ('sS', E.c['sS'])
                        w2rot[0] += 1
                for tt4 in range(4):
                    for hb in range(2):
                        rot = w2rot[0]
                        half = (rot % 2) * 512
                        psa_wait(rot)
                        for kc in range(HC):
                            E.t(lambda e: e.matmul(
                                psA[:, half:half + 512],
                                lnfm[:, gbase + kc * 512 + tt4 * 128:
                                     gbase + kc * 512 + tt4 * 128 + 128],
                                wv_sb[:, kc * H + hb * 512: kc * H + hb * 512 + 512],
                                start=(kc == 0), stop=(kc == HC - 1)),
                                inc=(kc == HC - 1))
                        tsn = E.c['sT']
                        E.wait('vector', 'sT', tsn)
                        mt = j * 4 + tt4
                        E.v(lambda e: e.scalar_tensor_tensor(
                            v_sb[:, mt * H + hb * 512: mt * H + hb * 512 + 512],
                            psA[:, half:half + 512], 1.0,
                            bv_sb[:, hb * 512:hb * 512 + 512],
                            ALU.mult, ALU.add), inc=True)
                        w2free[rot] = ('sV', E.c['sV'])
                        w2rot[0] += 1
                sn['mmT', b, j] = E.c['sT']

            def smax_exp(b, lt):
                par = lt % 2
                E.wait('vector', 'sT', sn['sc', b, lt])
                E.wait('vector', 'sS', sn.get(('exp', b, lt - 2), 0))
                dcol = lt * 128
                E.vser(lambda e: e.tensor_tensor(psS[:, dcol:dcol + 128],
                                                 psS[:, dcol:dcol + 128],
                                                 negeye_sb[:], ALU.add))
                E.vser(lambda e: e.reduce_max(rmaxb[:, par:par + 1], psS[:],
                                              AX.X))
                E.vser(lambda e: e.tensor_scalar(negb[:, par:par + 1],
                                                 rmaxb[:, par:par + 1],
                                                 -1.0 / 1024.0, None, ALU.mult))
                sn['smax', b, lt] = E.c['sV']
                E.wait('scalar', 'sV', sn['smax', b, lt])
                ptprev = sn.get(('pt', b, lt - 2), sn.get(('pt', b - 1, 14 + lt), 0))
                E.wait('scalar', 'sT', ptprev)
                E.s(lambda e: e.activation(P_sb[:, par * 2048:(par + 1) * 2048],
                                           psS[:], AF.Exp,
                                           bias=negb[:, par:par + 1],
                                           scale=1.0 / 1024.0,
                                           accum_out=rsacc[:, par:par + 1]),
                    inc=True)
                sn['exp', b, lt] = E.c['sS']

            def scores_stage(b, lt):
                if lt < LT:
                    gi = b * 16 + lt
                    slot = lt % 4
                    if lt >= 4:
                        E.wait('sync', 'sV', sn.get(('x1ev', b, lt - 4), 0))
                    else:
                        E.wait('sync', 'sS', sn.get(('ln1', b, 12 + lt), 0))
                    ld = E.dS(f'qX{slot}', xbuf[:, slot * H:(slot + 1) * H],
                              d_x[gi * 128:(gi + 1) * 128, :])
                    sn['ldxr', b, lt] = ld
                    E.wait('tensor', 'sS', sn.get(('exp', b, lt - 1), 0))
                    E.wait('tensor', 'sV', sn.get(('smax', b, lt - 1), 0))
                    if lt == 0:
                        E.wait('tensor', 'sV', sn['qkvdoneV', b])
                        E.wait('tensor', 'sS', sn['qkvdoneS', b])
                    for oc in range(HC):
                        for nb in range(4):
                            E.t(lambda e: e.matmul(
                                psS[:, nb * 512:(nb + 1) * 512],
                                q_sb[:, oc * L + lt * 128: oc * L + lt * 128 + 128],
                                k_sb[:, oc * L + nb * 512: oc * L + (nb + 1) * 512],
                                start=(oc == 0), stop=(oc == HC - 1)),
                                inc=(oc == HC - 1 and nb == 3))
                    sn['sc', b, lt] = E.c['sT']
                    if lt == 0:
                        smax_exp(b, 0)
                if lt >= 1:
                    j = lt - 1
                    jp = j % 2
                    for rnd in range(2):
                        pp = ptrot[0] % 2
                        E.wait('tensor', 'sS', sn['exp', b, j])
                        if ptrot[0] - 2 in ptfree:
                            E.wait('tensor', 'sV', ptfree[ptrot[0] - 2])
                        for s in range(8):
                            mt = rnd * 8 + s
                            E.t(lambda e: e.transpose(
                                psPT[pp][:, s * 128:(s + 1) * 128],
                                P_sb[:, jp * 2048 + mt * 128:
                                     jp * 2048 + (mt + 1) * 128],
                                eye_sb[:]), inc=(s == 7))
                        tsn = E.c['sT']
                        E.wait('vector', 'sT', tsn)
                        avprev = sn.get(('av', b, j - 1),
                                        sn.get(('av', b - 1, LT - 1), 0))
                        E.wait('vector', 'sT', avprev)
                        E.v(lambda e: e.tensor_copy(
                            pt_sb[:, rnd * 1024:(rnd + 1) * 1024],
                            psPT[pp][:]), inc=True)
                        ptfree[ptrot[0]] = E.c['sV']
                        ptrot[0] += 1
                    sn['pt', b, j] = E.c['sT']
                    ptall = E.c['sV']
                    E.wait('vector', 'sS', sn['exp', b, j])
                    E.v(lambda e: e.reciprocal(recipb[:, jp:jp + 1],
                                               rsacc[:, jp:jp + 1]), inc=True)
                    sn['recip', b, j] = E.c['sV']
                    if lt < LT:
                        smax_exp(b, lt)
                    E.wait('tensor', 'sV', ptall)
                    E.wait('tensor', 'sV', sn.get(('x1ev', b, j - 1),
                                                  sn.get(('x1ev', b - 1, LT - 1), 0)))
                    for hb in range(2):
                        for mt in range(LT):
                            E.t(lambda e: e.matmul(
                                psA[:, hb * 512:(hb + 1) * 512],
                                pt_sb[:, mt * 128:(mt + 1) * 128],
                                v_sb[:, mt * H + hb * 512: mt * H + hb * 512 + 512],
                                start=(mt == 0), stop=(mt == LT - 1)),
                                inc=(mt == LT - 1 and hb == 1))
                    sn['av', b, j] = E.c['sT']
                    jslot = j % 4
                    E.wait('vector', 'sT', sn['av', b, j])
                    E.wait('vector', 'sV', sn['recip', b, j])
                    E.wait('vector', f'qX{jslot}', sn['ldxr', b, j])
                    E.wait('vector', f'qSP{jp}', sn.get(('spill', b, j - 2), 0))
                    for hb in range(2):
                        E.v(lambda e: e.scalar_tensor_tensor(
                            x1_sb[:, jp * H + hb * 512: jp * H + hb * 512 + 512],
                            psA[:, hb * 512:(hb + 1) * 512],
                            recipb[:, jp:jp + 1],
                            xbuf[:, jslot * H + hb * 512: jslot * H + hb * 512 + 512],
                            ALU.mult, ALU.add), inc=(hb == 1))
                    sn['x1ev', b, j] = E.c['sV']
                    x1t = x1_sb[:, jp * H:(jp + 1) * H]
                    E.wait('vector', 'sV', sn['x1ev', b, j])
                    E.v(lambda e: e.reduce_sum(stw[:, 16:17], x1t, AX.X), inc=True)
                    E.wait('scalar', 'sV', sn['x1ev', b, j])
                    E.sser(lambda e: e.activation(sqdead[:], x1t, AF.Square,
                                                  accum_out=stw[:, 17:18]))
                    sn['sq2', b, j] = E.c['sS']
                    sn['rs2', b, j] = E.c['sV']
                    m22, esq2, var2 = stw[:, 18:19], stw[:, 19:20], stw[:, 20:21]
                    lnv2, rstd2, negm2 = stw[:, 21:22], stw[:, 22:23], stw[:, 23:24]
                    E.wait('scalar', 'sV', sn['rs2', b, j])
                    E.sser(lambda e: e.activation(m22, stw[:, 16:17], AF.Square,
                                                  scale=1.0 / H))
                    E.sser(lambda e: e.activation(esq2, stw[:, 17:18], AF.Identity,
                                                  bias=0.0, scale=1.0 / H))
                    sms2 = E.c['sS']
                    E.wait('vector', 'sS', sms2)
                    E.vser(lambda e: e.tensor_tensor(var2, esq2, m22, ALU.subtract))
                    svv2 = E.c['sV']
                    E.wait('scalar', 'sV', svv2)
                    E.sser(lambda e: e.activation(lnv2, var2, AF.Ln, bias=EPS))
                    E.sser(lambda e: e.activation(rstd2, lnv2, AF.Exp, scale=-0.5))
                    srs2 = E.c['sS']
                    E.wait('vector', 'sS', srs2)
                    E.vser(lambda e: e.scalar_tensor_tensor(
                        negm2, stw[:, 16:17], -1.0 / H, rstd2, ALU.mult, ALU.mult))
                    sn['ch2', b, j] = E.c['sV']
                    E.wait('scalar', 'sV', sn['ch2', b, j])
                    E.wait('scalar', f'qSP{jp}', sn.get(('spill', b, j - 2), 0))
                    E.s(lambda e: e.activation(ln2_sb[:, jp * H:(jp + 1) * H], x1t,
                                               AF.Identity, bias=negm2, scale=rstd2),
                        inc=True)
                    sn['ln2', b, j] = E.c['sS']
                    E.wait('vector', 'sS', sn['ln2', b, j])
                    E.vser(lambda e: e.reduce_max(stw[:, 28:29],
                                                  ln2_sb[:, jp * H:(jp + 1) * H],
                                                  AX.X, apply_absolute_value=True))
                    E.vser(lambda e: e.tensor_tensor(m0acc[:], m0acc[:],
                                                     stw[:, 28:29], ALU.max))
                    gi = b * 16 + j
                    E.wait('gpsimd', 'sV', sn['x1ev', b, j])
                    E.dG(f'qSP{jp}', d_x1[gi * 128:(gi + 1) * 128, :], x1t)
                    E.wait('gpsimd', 'sS', sn['ln2', b, j])
                    sp = E.dG(f'qSP{jp}', d_ln2[gi * 128:(gi + 1) * 128, :],
                              ln2_sb[:, jp * H:(jp + 1) * H])
                    sn['spill', b, j] = sp
                    if stage == 'dbg2' and b == 0 and j == 0:
                        E.wait('gpsimd', 'sS', E.c['sS'])
                        E.wait('gpsimd', 'sV', E.c['sV'])
                        off = [0]

                        def nxt(c):
                            o = off[0]
                            off[0] += 128 * c
                            return ydump(o, c)
                        E.dG('qPre', nxt(2048), P_sb[:, 0:2048])
                        E.dG('qPre', nxt(2048), pt_sb[:])
                        E.dG('qPre', nxt(32), stw[:, 0:32])
                        E.dG('qPre', nxt(2), rmaxb[:])
                        E.dG('qPre', nxt(2), rsacc[:])
                        E.dG('qPre', nxt(2), recipb[:])
                        E.dG('qPre', nxt(2048), x1_sb[:])
                    if stage in ('x1', 'ln2'):
                        if stage == 'x1':
                            E.wait('sync', 'sV', sn['x1ev', b, j])
                            E.dS('qY0', d_y[gi * 128:(gi + 1) * 128, :], x1t)
                        else:
                            E.wait('sync', 'sS', sn['ln2', b, j])
                            E.dS('qY0', d_y[gi * 128:(gi + 1) * 128, :],
                                 ln2_sb[:, jp * H:(jp + 1) * H])

            for b in range(BPC):
                if b == 0:
                    for w_sb, dsrc in ((wq_sb, d_wqt), (wk_sb, d_wkt),
                                       (wv_sb, d_wvt)):
                        lw = E.dG('qWQ',
                                  w_sb[:].rearrange("p (c o) -> p c o", c=HC),
                                  dsrc.rearrange("(c p) o -> p c o", p=128))
                sn['wqkv', b] = lw
                for g in range(5):
                    if g >= 1:
                        qkv_mms(b, g - 1)
                    if g < 4:
                        qkv_group(b, g)
                sn['qkvdoneS', b] = E.c['sS']
                sn['qkvdoneV', b] = E.c['sV']
                if stage == 'dbg1' and b == 0:
                    E.wait('gpsimd', 'sS', E.c['sS'])
                    E.wait('gpsimd', 'sV', E.c['sV'])
                    if T * H >= 4 * 128 * 16384:
                        E.dG('qPre', ydump(0, 16384), q_sb[:])
                        E.dG('qPre', ydump(128 * 16384, 16384), k_sb[:])
                        E.dG('qPre', ydump(2 * 128 * 16384, 16384), v_sb[:])
                        E.dG('qPre', ydump(3 * 128 * 16384, 4096), lnfm[:])
                    else:
                        E.dG('qPre', ydump(0, 4096), lnfm[:])
                        E.dG('qPre', ydump(128 * 4096, 8192), q_sb[:, 0:8192])
                    E.wait('sync', 'qPre', E.c['qPre'])
                    return
                for lt in range(LT + 1):
                    scores_stage(b, lt)
                if stage == 'dbg2' and b == 0:
                    E.barrier()
                    return

            if stage in ('x1', 'ln2'):
                E.barrier()
                return

            def allreduce_scalar(acc_ap, ar_idx, outs):
                E.wait('gpsimd', 'sV', E.c['sV'])
                E.dG('qCC', d_sc, acc_ap)
                E.wait('gpsimd', 'qCC', E.c['qCC'])
                ld = E.dG('qCC', row_sb[:],
                          d_sc.rearrange("p one -> (one p)").unsqueeze(0))
                E.wait('vector', 'qCC', ld)
                E.vser(lambda e: e.reduce_max(one_sb[0:1, 0:1], row_sb[:], AX.X))
                E.wait('gpsimd', 'sV', E.c['sV'])
                E.dG('qCC', d_ar_in[ar_idx][0:1, 0:1], one_sb[0:1, 0:1])
                E.wait('gpsimd', 'qCC', E.c['qCC'])
                E.g(lambda e: e.collective_compute(
                    "AllReduce", ALU.max, replica_groups=[list(range(NCORES))],
                    ins=[d_ar_in[ar_idx][0:1, 0:1].opt()],
                    outs=[d_ar_out[ar_idx][0:1, 0:1].opt()]),
                    inc='cc', k=1)
                E.wait('gpsimd', 'cc', E.c['cc'])
                ldb = E.dG('qCC', GBUF,
                           d_ar_out[ar_idx][0:1, 0:1].broadcast_to([128, 1]))
                E.wait('vector', 'qCC', ldb)
                s_ap, r_ap, qmax = outs
                E.vser(lambda e: e.tensor_scalar(s_ap, GBUF, 1.0 / qmax, 1e-12,
                                                 ALU.mult, ALU.max))
                E.vser(lambda e: e.reciprocal(r_ap, s_ap))
                sn['scale', ar_idx] = E.c['sV']

            def allreduce_pair(acc1, acc2, ar_idx, outs1, outs2):
                for slot, acc in ((0, acc1), (1, acc2)):
                    E.wait('gpsimd', 'sV', E.c['sV'])
                    E.dG('qCC', d_sc, acc)
                    E.wait('gpsimd', 'qCC', E.c['qCC'])
                    ld = E.dG('qCC', row_sb[:],
                              d_sc.rearrange("p one -> (one p)").unsqueeze(0))
                    E.wait('vector', 'qCC', ld)
                    E.vser(lambda e: e.reduce_max(one_sb[0:1, slot:slot + 1],
                                                  row_sb[:], AX.X))
                E.wait('gpsimd', 'sV', E.c['sV'])
                E.dG('qCC', d_ar_in[ar_idx][0:1, 0:2], one_sb[0:1, 0:2])
                E.wait('gpsimd', 'qCC', E.c['qCC'])
                E.g(lambda e: e.collective_compute(
                    "AllReduce", ALU.max, replica_groups=[list(range(NCORES))],
                    ins=[d_ar_in[ar_idx].opt()], outs=[d_ar_out[ar_idx].opt()]),
                    inc='cc', k=1)
                E.wait('gpsimd', 'cc', E.c['cc'])
                for slot, (s_ap, r_ap, qmax), gb in ((0, outs1, GBUF),
                                                     (1, outs2, GBUF2)):
                    E.wait('gpsimd', 'sV', E.c['sV'])
                    ldb = E.dG('qCC', gb, d_ar_out[ar_idx][0:1, slot:slot + 1]
                               .broadcast_to([128, 1]))
                    E.wait('vector', 'qCC', ldb)
                    E.vser(lambda e: e.tensor_scalar(s_ap, gb, 1.0 / qmax, 1e-12,
                                                     ALU.mult, ALU.max))
                    E.vser(lambda e: e.reciprocal(r_ap, s_ap))
                sn['scale', ar_idx] = E.c['sV']

            allreduce_scalar(m0acc[:], 0, (S0, R0, 32767.0))
            E.barrier()

            E.dS('qW', wu_sb[:].rearrange("p (c o) -> p c o", c=HC),
                 d_wut.transpose([1, 0, 2]))
            lw = E.dS('qW', wg_sb[:].rearrange("p (c o) -> p c o", c=HC),
                      d_wgt.transpose([1, 0, 2]))
            sn['wug'] = lw

            def b_prep(g):
                for tt4 in range(4):
                    ti = g * 4 + tt4
                    par = ti % 2
                    E.wait('sync', 'sV', sn.get(('xq', ti - 2), 0))
                    ld = E.dS(f'qLN{par}', lnbuf[:, par * H:(par + 1) * H],
                              d_ln2[ti * 128:(ti + 1) * 128, :])
                    E.wait('vector', f'qLN{par}', ld)
                    E.wait('vector', 'sT', sn.get(('xqtr', ti - 2), 0))
                    E.vser(lambda e: e.tensor_scalar(tmpq[:],
                                                     lnbuf[:, par * H:(par + 1) * H],
                                                     R0, MAGIC, ALU.mult, ALU.add))
                    E.vser(lambda e: e.tensor_scalar(xqtm[:, par * H:(par + 1) * H],
                                                     tmpq[:], MAGIC, S0,
                                                     ALU.subtract, ALU.mult))
                    sn['xq', ti] = E.c['sV']
                    pp = ptrot[0] % 2
                    E.wait('tensor', 'sV', sn['xq', ti])
                    if ptrot[0] - 2 in ptfree:
                        E.wait('tensor', 'sV', ptfree[ptrot[0] - 2])
                    for kc in range(HC):
                        E.t(lambda e: e.transpose(
                            psPT[pp][:, kc * 128:(kc + 1) * 128],
                            xqtm[:, par * H + kc * 128: par * H + (kc + 1) * 128],
                            eye_sb[:]), inc=(kc == HC - 1))
                    tsn = E.c['sT']
                    E.wait('vector', 'sT', tsn)
                    E.wait('vector', 'qXQS', sn.get(('xqsp', g - 1), 0))
                    dst3 = xqfm[:, 0:HC * 512].rearrange(
                        "p (kc f) -> p kc f", f=512)[
                        :, :, tt4 * 128:(tt4 + 1) * 128]
                    E.v(lambda e: e.tensor_copy(
                        dst3, psPT[pp][:].rearrange("p (c f) -> p c f", f=128)),
                        inc=True)
                    ptfree[ptrot[0]] = E.c['sV']
                    ptrot[0] += 1
                    sn['xqtr', ti] = E.c['sT']
                sn['xqfm', g] = E.c['sV']
                E.wait('gpsimd', 'sV', sn['xqfm', g])
                sp = E.dG('qXQS', d_xq[g], xqfm[:, 0:HC * 512])
                sn['xqsp', g] = sp

            def b_mms_one(j, w_sb, dsp, acc, rhs_base, ld_sem, ld_val,
                          hook=None):
                for ic in range(IC):
                    if hook is not None:
                        hook()
                    rot = ps6rot[0]
                    pb = ps6(rot)
                    ps6_wait(rot)
                    if ic == 0:
                        E.wait('tensor', ld_sem, ld_val)
                        if j == 0:
                            E.wait('tensor', 'qW', sn['wug'])
                    for kc in range(HC):
                        E.t(lambda e: e.matmul(
                            pb,
                            w_sb[:, kc * II + ic * 128: kc * II + ic * 128 + 128],
                            xqfm[:, rhs_base + kc * 512: rhs_base + (kc + 1) * 512],
                            start=(kc == 0), stop=(kc == HC - 1)),
                            inc=(kc == HC - 1))
                    tsn = E.c['sT']
                    srot = rot % 6
                    E.wait('scalar', 'sT', tsn)
                    E.wait('scalar', f'qSG{srot}', sn.get(('bspill', rot - 6), 0))
                    E.wait('scalar', 'sV', sn.get(('bred', rot - 6), 0))
                    E.s(lambda e: e.activation(stg[:, srot * 512:(srot + 1) * 512],
                                               pb, AF.Identity,
                                               bias=0.0, scale=1.0), inc=True)
                    ps6free[rot] = ('sS', E.c['sS'])
                    ev = E.c['sS']
                    E.wait('vector', 'sS', ev)
                    E.vser(lambda e: e.reduce_max(
                        stmx[:, srot:srot + 1],
                        stg[:, srot * 512:(srot + 1) * 512],
                        AX.X, apply_absolute_value=True))
                    sn['bred', rot] = E.c['sV']
                    if srot == 5 or ic == IC - 1:
                        lo = (srot // 6) * 6
                        E.vser(lambda e: e.reduce_max(redt[:, 0:1],
                                                      stmx[:, lo:srot + 1], AX.X))
                        E.vser(lambda e: e.tensor_tensor(acc[:], acc[:],
                                                         redt[:, 0:1], ALU.max))
                        sn['bred', rot] = E.c['sV']
                    E.wait('gpsimd', 'sS', ev)
                    sp = E.dG(f'qSG{srot}', dsp[ic * 128:(ic + 1) * 128,
                                                j * 512:(j + 1) * 512],
                              stg[:, srot * 512:(srot + 1) * 512])
                    sn['bspill', rot] = sp
                    ps6rot[0] += 1

            for g in range(NGRP + 1):
                if g >= 1:
                    b_mms_one(g - 1, wg_sb, d_g, gmaxa, 0, 'sV',
                              sn.get(('xqfm', g - 1), 0))
                    sn['bmmT', g - 1] = E.c['sT']
                if g < NGRP:
                    b_prep(g)
            sn['gspills'] = {i: E.c[f'qSG{i}'] for i in range(4)}

            allreduce_scalar(gmaxa[:], 1, (SG, RG, 32767.0))
            E.vser(lambda e: e.tensor_scalar(scl[:, 12:13], SG, 1.0 / 65536.0,
                                             None, ALU.mult))
            sn['sg216'] = E.c['sV']
            SG216 = scl[:, 12:13]

            def b2_load(g):
                par = (g + 1) % 2
                if g == 0:
                    E.wait('sync', 'qXQS', sn['xqsp', 0])
                elif g == 1:
                    E.wait('sync', 'sT', sn.get(('bmmT', NGRP - 1), 0))
                    E.wait('sync', 'qXQS', sn['xqsp', NGRP - 1])
                else:
                    E.wait('sync', 'sT', sn['b2mmT', g - 2])
                ld = E.dS(f'qXQ{par}', xqfm[:, par * HC * 512:(par + 1) * HC * 512],
                          d_xq[g])
                sn['b2ld', g] = ld

            def c_tile_gen(idx):
                ic, sl = idx // NSL, idx % NSL
                par = idx % 2
                if idx == 0:
                    for i4 in range(4):
                        E.wait('sync', f'qSG{i4}', sn['gspills'][i4])
                    E.wait('vector', 'sV', sn['sg216'])
                    E.wait('scalar', 'sV', sn['sg216'])
                gb = cbuf[:, par * 2048:(par + 1) * 2048]
                E.wait('sync', 'sV', sn.get(('cgq', idx - 2), 0))
                ld = E.dS(f'qCB{par}', gb, d_g[ic * 128:(ic + 1) * 128,
                                               sl * 2048:(sl + 1) * 2048])
                E.wait('scalar', f'qCB{par}', ld)
                E.wait('scalar', 'sV', sn.get(('cgq', idx - 2), 0))
                E.sser(lambda e: e.activation(sigb[:, par * 2048:(par + 1) * 2048],
                                              gb, AF.Identity, bias=MAGIC, scale=RG))
                sn['cc1', idx] = E.c['sS']
                yield
                E.wait('vector', 'sS', sn['cc1', idx])
                E.vser(lambda e: e.tensor_scalar(
                    c1[:], sigb[:, par * 2048:(par + 1) * 2048], MAGIC, SG216,
                    ALU.subtract, ALU.mult))
                sn['cgq', idx] = E.c['sV']
                yield
                E.wait('scalar', 'sV', sn['cgq', idx])
                E.wait('scalar', 'sV', sn.get(('ctV', idx - 1), 0))
                E.sser(lambda e: e.activation(c2[:], c1[:], AF.Sigmoid,
                                              scale=65536.0))
                yield
                E.sser(lambda e: e.activation(ngb[:], c2[:], AF.Identity,
                                              bias=MAGIC, scale=65536.0))
                sn['csig', idx] = E.c['sS']
                yield
                E.wait('vector', 'sS', sn['csig', idx])
                E.vser(lambda e: e.tensor_scalar(ngb[:], ngb[:], MAGIC, 65535.0,
                                                 ALU.subtract, ALU.min))
                sn['ckq', idx] = E.c['sV']
                yield
                tbv = tbb[:, par * 2048:(par + 1) * 2048]
                E.wait('vector', f'qTS{par}', sn.get(('ctsp', idx - 2), 0))
                E.vser(lambda e: e.tensor_tensor(tbv, c1[:], ngb[:], ALU.mult))
                yield
                E.vser(lambda e: e.reduce_max(redt[:, 0:1], tbv, AX.X,
                                              apply_absolute_value=True))
                E.vser(lambda e: e.tensor_tensor(tmaxa[:], tmaxa[:], redt[:, 0:1],
                                                 ALU.max))
                stv = E.c['sV']
                sn['ctV', idx] = stv
                E.wait('gpsimd', 'sV', stv)
                sp = E.dG(f'qTS{par}',
                          d_t[ic * 128:(ic + 1) * 128, sl * 2048:(sl + 1) * 2048],
                          tbv)
                sn['ctsp', idx] = sp
                yield

            def all_c_gen():
                for idx in range(IC * NSL):
                    yield from c_tile_gen(idx)
            cgen = [None]

            def emit_c_tiles(n):
                if cgen[0] is None:
                    cgen[0] = all_c_gen()
                for _ in range(n):
                    next(cgen[0], None)

            for g in range(NGRP + 1):
                if g >= 1:
                    j = g - 1
                    b_mms_one(j, wu_sb, d_u, umaxa, ((j + 1) % 2) * HC * 512,
                              f'qXQ{(j + 1) % 2}', sn['b2ld', j],
                              hook=lambda: emit_c_tiles(1))
                    sn['b2mmT', j] = E.c['sT']
                if g < NGRP:
                    b2_load(g)
            emit_c_tiles(IC * NSL * 8)

            allreduce_pair(umaxa[:], tmaxa[:], 2, (SU, RU, 32767.0),
                           (ST_, RT_, 32767.0))
            E.vser(lambda e: e.tensor_tensor(STSU, ST_, SU, ALU.mult))
            E.barrier()

            NW = 4096 if NSL % 2 == 0 else 2048
            nh = max(1, (NSL * 2048) // NW)
            for idx in range(IC * nh):
                ic, s2 = idx // nh, idx % nh
                par = idx % 2
                c0 = s2 * NW
                tbv = tb2[:, par * NW:(par + 1) * NW]
                ubv = ub2[:, par * NW:(par + 1) * NW]
                cb = cb2[:, par * NW:(par + 1) * NW]
                ub1 = sg2[:, par * NW:(par + 1) * NW]
                ntv = nt2[:, par * NW:(par + 1) * NW]
                nuv = nu2[:, par * NW:(par + 1) * NW]
                hw2 = NW // 2
                E.wait('sync', 'sS', sn.get(('dtS', idx - 2), 0))
                ldta = E.dS(f'qTB{par}', tbv[:, 0:hw2],
                            d_t[ic * 128:(ic + 1) * 128, c0:c0 + hw2])
                ldtb = E.dS(f'qTC{par}', tbv[:, hw2:NW],
                            d_t[ic * 128:(ic + 1) * 128, c0 + hw2:c0 + NW])
                E.wait('sync', 'sS', sn.get(('duS', idx - 2), 0))
                ldua = E.dS(f'qUB{par}', ubv[:, 0:hw2],
                            d_u[ic * 128:(ic + 1) * 128, c0:c0 + hw2])
                ldub = E.dS(f'qUC{par}', ubv[:, hw2:NW],
                            d_u[ic * 128:(ic + 1) * 128, c0 + hw2:c0 + NW])
                E.wait('scalar', f'qTB{par}', ldta)
                E.wait('scalar', f'qTC{par}', ldtb)
                E.wait('scalar', 'sV', sn.get(('dzv', idx - 2), 0))
                E.wait('scalar', f'qNT{par}', sn.get(('dzsp', idx - 2), 0))
                E.wait('scalar', f'qNC{par}', sn.get(('dzsp2', idx - 2), 0))
                E.sser(lambda e: e.activation(cb, tbv, AF.Identity, bias=MAGIC,
                                              scale=RT_))
                sn['dtS', idx] = E.c['sS']
                E.wait('scalar', f'qUB{par}', ldua)
                E.wait('scalar', f'qUC{par}', ldub)
                E.sser(lambda e: e.activation(ub1, ubv, AF.Identity, bias=MAGIC,
                                              scale=RU))
                E.sser(lambda e: e.activation(nuv, ub1, AF.Identity, bias=-MAGIC,
                                              scale=1.0))
                sn['duS', idx] = E.c['sS']
                E.wait('vector', 'sS', sn['dtS', idx])
                E.vser(lambda e: e.tensor_scalar(ntv, cb, MAGIC, STSU,
                                                 ALU.subtract, ALU.mult))
                E.wait('vector', 'sS', sn['duS', idx])
                E.vser(lambda e: e.tensor_tensor(cb, ntv, nuv, ALU.mult))
                E.vser(lambda e: e.reduce_max(redt[:, 1:2], cb, AX.X,
                                              apply_absolute_value=True))
                E.vser(lambda e: e.tensor_tensor(zmaxa[:], zmaxa[:], redt[:, 1:2],
                                                 ALU.max))
                sn['dzv', idx] = E.c['sV']
                E.wait('gpsimd', 'sV', sn['dzv', idx])
                spza = E.dG(f'qNT{par}',
                            d_z[ic * 128:(ic + 1) * 128, c0:c0 + hw2],
                            cb[:, 0:hw2])
                spzb = E.dG(f'qNC{par}',
                            d_z[ic * 128:(ic + 1) * 128, c0 + hw2:c0 + NW],
                            cb[:, hw2:NW])
                sn['dzsp', idx] = spza
                sn['dzsp2', idx] = spzb
                
            allreduce_scalar(zmaxa[:], 3, (SZ, RZ, 32767.0))
            E.barrier()

            lw = E.dS('qW', wd_sb[:].rearrange("p (c o) -> p c o", c=IC),
                      d_wdt.transpose([1, 0, 2]))
            sn['wd'] = lw

            def e_prep_group(sl, g4):
                third = g4 // 2
                if g4 % 2 == 0:
                    hidx = sl * 3 + third
                    hpar = hidx % 2
                    r0 = third * 8 * 128
                    E.wait('gpsimd', 'sS', sn.get(('ehalf', hidx - 2), 0))
                    ldz = E.dG(f'qEN{hpar}',
                               zb_e[:, hpar * 4096:(hpar + 1) * 4096]
                               .rearrange("p (c t) -> p c t", t=512),
                               d_z[r0:r0 + 8 * 128, sl * 512:(sl + 1) * 512]
                               .rearrange("(c p) t -> p c t", p=128))
                    sn['eld', hidx] = ldz
                hidx = sl * 3 + third
                hpar = hidx % 2
                gidx = sl * 6 + g4
                gpar = gidx % 2
                inoff = hpar * 4096 + (g4 % 2) * 2048
                eos = eo1[:, gpar * 2048:(gpar + 1) * 2048]
                E.wait('scalar', f'qEN{hpar}', sn['eld', hidx])
                E.wait('scalar', 'sV', sn.get(('eo', gidx - 2), 0))
                E.sser(lambda e: e.activation(eos, zb_e[:, inoff:inoff + 2048],
                                              AF.Identity, bias=MAGIC, scale=RZ))
                sez = E.c['sS']
                if g4 % 2 == 1:
                    sn['ehalf', hidx] = sez
                E.wait('vector', 'sS', sez)
                E.wait('vector', 'sT', sn.get(('emm', sl - 2), 0))
                E.vser(lambda e: e.tensor_scalar(
                    ofm[:, (sl % 2) * IC * 512 + g4 * 2048:
                        (sl % 2) * IC * 512 + (g4 + 1) * 2048],
                    eos, MAGIC, SZ, ALU.subtract, ALU.mult))
                sn['eo', gidx] = E.c['sV']
                if g4 == 5:
                    sn['eofm', sl] = E.c['sV']

            def e_mms_t(j, tt4):
                obase = (j % 2) * IC * 512
                if True:
                    ti = j * 4 + tt4
                    parx = ti % 2
                    E.wait('sync', f'qY{parx}', sn.get(('yst', ti - 2), 0))
                    ldx = E.dS(f'qXB{parx}', x1b[:, parx * H:(parx + 1) * H],
                               d_x1[ti * 128:(ti + 1) * 128, :])
                    for hb in range(2):
                        rot = ps6rot[0]
                        pb = ps6(rot)
                        ps6_wait(rot)
                        E.wait('tensor', 'sV', sn['eofm', j])
                        if j == 0 and tt4 == 0 and hb == 0:
                            E.wait('tensor', 'qW', sn['wd'])
                        for ic in range(IC):
                            E.t(lambda e: e.matmul(
                                pb,
                                ofm[:, obase + ic * 512 + tt4 * 128:
                                    obase + ic * 512 + tt4 * 128 + 128],
                                wd_sb[:, ic * H + hb * 512: ic * H + hb * 512 + 512],
                                start=(ic == 0), stop=(ic == IC - 1)),
                                inc=(ic == IC - 1))
                        tsn = E.c['sT']
                        E.wait('vector', 'sT', tsn)
                        E.wait('vector', f'qXB{parx}', ldx)
                        E.wait('vector', f'qY{parx}', sn.get(('yst', ti - 2), 0))
                        E.v(lambda e: e.scalar_tensor_tensor(
                            y_sb[:, parx * H + hb * 512: parx * H + hb * 512 + 512],
                            pb, 1.0,
                            x1b[:, parx * H + hb * 512: parx * H + hb * 512 + 512],
                            ALU.mult, ALU.add), inc=True)
                        ps6free[rot] = ('sV', E.c['sV'])
                        ps6rot[0] += 1
                    yv = E.c['sV']
                    E.wait('sync', 'sV', yv)
                    st = E.dS(f'qY{parx}', d_y[ti * 128:(ti + 1) * 128, :],
                              y_sb[:, parx * H:(parx + 1) * H])
                    sn['yst', ti] = st
                if tt4 == 3:
                    sn['emm', j] = E.c['sT']

            for sl in range(NGRP + 1):
                steps = []
                if sl < NGRP:
                    steps += [('p', sl, 0), ('p', sl, 1)]
                if sl >= 1:
                    steps += [('m', sl - 1, 0)]
                if sl < NGRP:
                    steps += [('p', sl, 2)]
                if sl >= 1:
                    steps += [('m', sl - 1, 1)]
                if sl < NGRP:
                    steps += [('p', sl, 3)]
                if sl >= 1:
                    steps += [('m', sl - 1, 2)]
                if sl < NGRP:
                    steps += [('p', sl, 4), ('p', sl, 5)]
                if sl >= 1:
                    steps += [('m', sl - 1, 3)]
                for kind, a, b2 in steps:
                    if kind == 'p':
                        e_prep_group(a, b2)
                    else:
                        e_mms_t(a, b2)

            E.barrier()

        with nc.Block() as block:
            block.sync(lambda eng: walk(Em('sync', eng, sems)))
            block.gpsimd(lambda eng: walk(Em('gpsimd', eng, sems)))
            block.tensor(lambda eng: walk(Em('tensor', eng, sems)))
            block.vector(lambda eng: walk(Em('vector', eng, sems)))
            block.scalar(lambda eng: walk(Em('scalar', eng, sems)))

    return nc


def kernel(L0_attn_x, L0_attn_Wq, L0_attn_bq, L0_attn_Wk, L0_attn_bk,
           L0_attn_Wv, L0_attn_bv, L0_ffn_w_gate, L0_ffn_w_up, L0_ffn_w_down):
    stage = os.environ.get("ATH_STAGE", "")
    f32 = np.float32
    x = np.asarray(L0_attn_x, dtype=f32)
    wqt = np.ascontiguousarray(np.asarray(L0_attn_Wq, dtype=f32).T)
    wkt = np.ascontiguousarray(np.asarray(L0_attn_Wk, dtype=f32).T)
    wvt = np.ascontiguousarray(np.asarray(L0_attn_Wv, dtype=f32).T)
    bq2 = np.ascontiguousarray(np.asarray(L0_attn_bq, dtype=f32).reshape(HC, 128).T)
    bk2 = np.ascontiguousarray(np.asarray(L0_attn_bk, dtype=f32).reshape(HC, 128).T)
    bv_row = np.ascontiguousarray(np.asarray(L0_attn_bv, dtype=f32).reshape(1, H))
    w_up = np.ascontiguousarray(np.asarray(L0_ffn_w_up, dtype=f32))
    w_gate = np.ascontiguousarray(np.asarray(L0_ffn_w_gate, dtype=f32))
    w_down = np.ascontiguousarray(np.asarray(L0_ffn_w_down, dtype=f32))
    eye = np.eye(128, dtype=f32)
    negeye = np.asarray(np.eye(128) * NEG_BIG, dtype=f32)

    nc = build_program(stage)
    in_maps = []
    for c in range(NCORES):
        xs = np.ascontiguousarray(x[c * BPC:(c + 1) * BPC].reshape(T, H))
        in_maps.append(dict(x=xs, wqt=wqt, wkt=wkt, wvt=wvt, bq2=bq2, bk2=bk2,
                            bv_row=bv_row, w_up=w_up, w_gate=w_gate,
                            w_down=w_down, eye=eye, negeye=negeye))
    res = run_bass_kernel_spmd(nc, in_maps, core_ids=list(range(NCORES)))
    y = np.concatenate([r["y"].reshape(BPC, L, H) for r in res.results], axis=0)
    return y
